# version 28
# baseline (speedup 1.0000x reference)
import numpy as np
import ml_dtypes
from contextlib import ExitStack

import concourse.bass as bass
import concourse.mybir as mybir
from concourse.bass_utils import run_bass_kernel_spmd

F32 = mybir.dt.float32
BF16 = mybir.dt.bfloat16
I32 = mybir.dt.int32
AF = mybir.ActivationFunctionType
ALU = mybir.AluOpType
AX = mybir.AxisListType

S = 4096
D = 1024
NT = S // 128
H = 16
DK = 64
HID = 3072
C_MAGIC = float(3 << 22)
BASS_DEBUG = False
DEBUG_TILE = -1
DEBUG_L1 = False
LAUNCH_PLAN = ("full",)
VST = 65
NQB = 8
HV = H * VST


class Sem:
    def __init__(self, nc, es, name, owner=""):
        self.h = es.enter_context(nc.semaphore(name))
        self.n = 0
        self.owner = owner

    def inc(self, instr, k=1):
        instr.then_inc(self.h, k)
        self.n += k
        return instr


class _MockInstr:
    def then_inc(self, sem, k=1, **kw):
        return self

    def annotate(self, *a, **kw):
        return self


class _MockEngine:
    _MI = _MockInstr()

    def __getattr__(self, name):
        def f(*a, **kw):
            return _MockEngine._MI
        return f


_MOCK = _MockEngine()


class Ctx:

    def __init__(self, nc, block, es):
        self.nc, self.block, self.es = nc, block, es
        self.LD = Sem(nc, es, "sLD", "sync")
        self.LDW = Sem(nc, es, "sLDW", "sync")
        self.GD = Sem(nc, es, "sGD", "gpsimd")
        self.GDS = Sem(nc, es, "sGDS", "gpsimd")
        self.PE = Sem(nc, es, "sPE", "tensor")
        self.ACT = Sem(nc, es, "sACT", "scalar")
        self.DVE = Sem(nc, es, "sDVE", "vector")
        self.PL = Sem(nc, es, "sPL", "gpsimd")
        self.queues = {"sync": [], "tensor": [], "vector": [], "scalar": [],
                       "gpsimd": []}

    MERGE = ("sync", "tensor", "vector", "scalar", "gpsimd")
    CHUNK = {}

    def _submit(self, name, f):
        if name in Ctx.MERGE:
            self.queues[name].append(f)
            f(_MOCK)
        else:
            getattr(self.block, name)(f)

    def sy(self, f):
        self._submit("sync", f)

    def te(self, f):
        self._submit("tensor", f)

    def ve(self, f):
        self._submit("vector", f)

    def sc(self, f):
        self._submit("scalar", f)

    def gp(self, f):
        self._submit("gpsimd", f)

    def flush(self):
        if not Ctx.MERGE:
            return
        for sem in (self.LD, self.LDW, self.GD, self.GDS, self.PE, self.ACT,
                    self.DVE, self.PL):
            if sem.owner in Ctx.MERGE:
                sem.n = 0
        for name in Ctx.MERGE:
            fs = self.queues[name]
            chunk = Ctx.CHUNK.get(name, len(fs)) or len(fs)
            for j in range(0, len(fs), chunk):
                part = fs[j:j + chunk]

                def body(eng, part=part):
                    for f in part:
                        f(eng)
                getattr(self.block, name)(body)

    def marks(self):
        return {"PE": self.PE.n, "ACT": self.ACT.n, "DVE": self.DVE.n,
                "PL": self.PL.n}


def wait_marks(c, eng, marks):
    if marks is None:
        return
    for name, val in marks.items():
        sem = getattr(c, name)
        if val > 0:
            eng.wait_ge(sem.h, val)



def emit_layernorm(c, x_t, xn_bf, b, joins, extra_ld=None,
                   vec_waits=(), sca_waits=()):
    mean, var, msq, rstd, sq = b["mean"], b["var"], b["msq"], b["rstd"], b["sq"]
    nmean = b["nmean"]

    def vec(v):
        wait_marks(c, v, joins)
        for sem, val in vec_waits:
            v.wait_ge(sem.h, val)
        if extra_ld is not None:
            v.wait_ge(c.LD.h, extra_ld)
        c.DVE.inc(v.reduce_sum(mean[:], x_t, AX.X))
        v.drain()
        v.scalar_tensor_tensor(msq[:], mean[:], 1.0 / (D * D), mean[:],
                               ALU.mult, ALU.mult)
        c.DVE.inc(v.tensor_scalar(nmean[:], mean[:], -1.0 / D, None,
                                  ALU.mult))
        v.drain()
    c.ve(vec)

    def sca(g):
        wait_marks(c, g, joins)
        for sem, val in sca_waits:
            g.wait_ge(sem.h, val)
        if extra_ld is not None:
            g.wait_ge(c.LD.h, extra_ld)
        c.PL.inc(g.tensor_tensor(sq[:], x_t, x_t, ALU.mult))
    c.gp(sca)
    sq_done = c.PL.n

    def vec2(v, sq_done=sq_done):
        v.wait_ge(c.PL.h, sq_done)
        c.DVE.inc(v.reduce_sum(var[:], sq[:], AX.X))
        v.drain()
        c.DVE.inc(v.tensor_scalar(var[:], var[:], 1.0 / D, 1e-5, ALU.mult,
                                  ALU.add))
        v.drain()
        c.DVE.inc(v.tensor_sub(var[:], var[:], msq[:]))
    c.ve(vec2)
    var_done = c.DVE.n

    def sca2(s, var_done=var_done):
        s.wait_ge(c.DVE.h, var_done)
        c.ACT.inc(s.activation(rstd[:], var[:], AF.Sqrt))
    c.sc(sca2)
    std_done = c.ACT.n

    def vec3(v, std_done=std_done):
        v.wait_ge(c.ACT.h, std_done)
        c.DVE.inc(v.reciprocal(b["rstd2"][:], rstd[:]))
        v.drain()
        c.DVE.inc(v.tensor_scalar(xn_bf[:], x_t, nmean[:, 0:1],
                                  b["rstd2"][:, 0:1], ALU.add, ALU.mult))
    c.ve(vec3)
    return c.DVE.n


def emit_transpose8(c, src_bf, pt_tr, ident, dst_slice_fn, pe_wait=None,
                    evac_wait=None):
    def ten(t):
        for sem, val in (pe_wait or []):
            t.wait_ge(sem.h, val)
        for ch in range(8):
            i = t.transpose(pt_tr[:, 128 * ch:128 * ch + 128],
                            src_bf[:, 128 * ch:128 * ch + 128], ident[:])
        c.PE.inc(i)
    c.te(ten)
    tr_done = c.PE.n

    def sca(s, tr_done=tr_done):
        s.wait_ge(c.PE.h, tr_done)
        for sem, val in (evac_wait or []):
            s.wait_ge(sem.h, val)
        dst, src = dst_slice_fn()
        c.ACT.inc(s.activation(dst, src, AF.Copy))
    c.sc(sca)
    return c.ACT.n


def emit_quant_fp6(c, src_bf, xd_bf, qb, consts, dve_wait, joins=None):
    am, scale_t, inv = qb["am"], qb["scale"], qb["inv"]
    y, c2, c4, r1, r2, r3, ra = (qb["y"], qb["c2"], qb["c4"], qb["r1"],
                                 qb["r2"], qb["r3"], qb["ra"])
    Ccol, nC8, nC4, nC2 = (consts["Ccol"], consts["nC8"], consts["nC4"],
                           consts["nC2"])
    x3 = src_bf[:].rearrange("p (b k) -> p b k", k=32)
    y3 = y.rearrange("p (b k) -> p b k", k=32) if hasattr(y, "rearrange") else None

    def vec(v):
        wait_marks(c, v, joins)
        v.wait_ge(c.DVE.h, dve_wait)
        c.DVE.inc(v.tensor_reduce(am[:], x3, AX.X, ALU.max,
                                  apply_absolute_value=True))
    c.ve(vec)
    am_done = c.DVE.n

    def sca0(sx, am_done=am_done):
        sx.wait_ge(c.DVE.h, am_done)
        c.ACT.inc(sx.activation(scale_t[:], am[:], AF.Identity,
                                bias=consts["eC"][:, 0:1], scale=1.0 / 7.5))
    c.sc(sca0)
    inv_done = c.ACT.n

    def vecb(v, inv_done=inv_done):
        v.wait_ge(c.ACT.h, inv_done)
        c.DVE.inc(v.reciprocal(inv[:], scale_t[:]))
        v.drain()
        inv3 = inv[:].rearrange("p (b o) -> p b o", o=1).broadcast_to(
            [128, 32, 32])
        yv = y.rearrange("p (b k) -> p b k", k=32)
        c.DVE.inc(v.tensor_tensor(yv, x3, inv3, ALU.mult))
        v.drain()
        v.tensor_scalar(c2, y, 2.0, -2.0, ALU.min, ALU.max)
        c.DVE.inc(v.tensor_scalar(c4, y, 4.0, -4.0, ALU.min, ALU.max))
    c.ve(vecb)
    clamps = c.DVE.n

    def pool(g, clamps=clamps):
        wait_marks(c, g, joins)
        g.wait_ge(c.DVE.h, clamps)
        c.PL.inc(g.tensor_sub(r2, c4, c2))
        c.PL.inc(g.tensor_sub(r3, y, c4))
    c.gp(pool)
    e2_done, e3_done = c.PL.n - 1, c.PL.n

    rb, rc = consts["rb"], consts["rc"]

    def sca(s, clamps=clamps, e2_done=e2_done, e3_done=e3_done):
        s.wait_ge(c.DVE.h, clamps)
        s.activation(ra, c2, AF.Identity, bias=Ccol[:, 0:1], scale=8.0)
        s.wait_ge(c.PL.h, e2_done)
        s.activation(rb, r2, AF.Identity, bias=Ccol[:, 0:1], scale=4.0)
        s.wait_ge(c.PL.h, e3_done)
        c.ACT.inc(s.activation(rc, r3, AF.Identity, bias=Ccol[:, 0:1],
                               scale=2.0))
        s.drain()
        s.activation(r1, ra, AF.Identity, bias=nC8[:, 0:1], scale=0.125)
        s.activation(r2, rb, AF.Identity, bias=nC4[:, 0:1], scale=0.25)
        c.ACT.inc(s.activation(r3, rc, AF.Identity, bias=nC2[:, 0:1],
                               scale=0.5))
    c.sc(sca)
    r_done = c.ACT.n

    def vec2(v, r_done=r_done):
        v.wait_ge(c.ACT.h, r_done)
        c.DVE.inc(v.tensor_add(y, r1, r2))
        v.drain()
        c.DVE.inc(v.tensor_add(y, y, r3))
        v.drain()
        sc3 = scale_t[:].rearrange("p (b o) -> p b o", o=1).broadcast_to(
            [128, 32, 32])
        yv = y.rearrange("p (b k) -> p b k", k=32)
        xdv = xd_bf[:].rearrange("p (b k) -> p b k", k=32)
        c.DVE.inc(v.tensor_tensor(xdv, yv, sc3, ALU.mult))
    c.ve(vec2)
    return c.DVE.n


def emit_negm_bcast(c, m16_src, m16_out, nm_bf, nmrow, pt_small, psmall,
                    negm_bc, ident, ones_row, sqrt_scale, dve_wait,
                    bcm_waits=(), af=None):
    if af is None:
        af = AF.Sqrt
    def sca(s):
        s.wait_ge(c.DVE.h, dve_wait)
        c.ACT.inc(s.activation(m16_out[:], m16_src[:], af,
                               scale=sqrt_scale))
    c.sc(sca)
    m_done = c.ACT.n

    def vec(v, m_done=m_done):
        v.wait_ge(c.ACT.h, m_done)
        c.DVE.inc(v.tensor_scalar(nm_bf[:], m16_out[:], -1.0, None, ALU.mult))
        v.drain()
    c.ve(vec)
    nm_done = c.DVE.n

    def ten(t, nm_done=nm_done):
        t.wait_ge(c.DVE.h, nm_done)
        c.PE.inc(t.transpose(pt_small[0:1, 0:16], nm_bf[:], ident[0:16, 0:16]))
    c.te(ten)
    t1 = c.PE.n

    def vec2(v, t1=t1):
        v.wait_ge(c.PE.h, t1)
        c.DVE.inc(v.tensor_copy(nmrow[:], pt_small[0:1, 0:16]))
        v.drain()
    c.ve(vec2)
    nr_done = c.DVE.n

    def ten2(t, nr_done=nr_done):
        t.wait_ge(c.DVE.h, nr_done)
        for sem, val in bcm_waits:
            t.wait_ge(sem.h, val)
        c.PE.inc(t.matmul(psmall[:, 0:16], ones_row[:], nmrow[:],
                          start=True, stop=True))
    c.te(ten2)
    t2 = c.PE.n

    def sca2(s, t2=t2):
        s.wait_ge(c.PE.h, t2)
        c.ACT.inc(s.activation(negm_bc[:], psmall[:, 0:16], AF.Copy))
    c.sc(sca2)
    return c.ACT.n



def build_l0(nc, c, io, es):
    x_in, w_qkv, w_o, fw, fb = io["x"], io["w_qkv"], io["w_o"], io["fw"], io["fb"]
    cosr, sinr = io["cosr"], io["sinr"]
    identc, m_oldc, m_diagc = io["ident"], io["m_old"], io["m_diag"]
    x1 = io["x1"]

    sb = lambda n, sh, dt: es.enter_context(nc.sbuf_tensor(n, sh, dt))
    ps = lambda n, sh, dt: es.enter_context(nc.psum_tensor(n, sh, dt))

    wqkv = sb("wqkv", [128, 8 * 3072], BF16)
    wo = sb("wo", [128, 8 * 1024], BF16)
    wdq = sb("wdq", [128, 8 * 1024], BF16)
    fbias = sb("fbias", [1, 1024], BF16)
    ident = sb("identS", [128, 128], BF16)
    sc_bf = sb("sc_bf", [128, 1024], F32)
    mbO = sb("mbO", [128, 128], BF16)
    mbD = sb("mbD", [128, 128], BF16)
    identF = sb("identF", [128, 128], F32)
    ones_row = sb("ones_row", [1, 128], BF16)
    Ccol = sb("Ccol", [128, 1], F32)
    nC8 = sb("nC8", [128, 1], F32)
    nC4 = sb("nC4", [128, 1], F32)
    nC2 = sb("nC2", [128, 1], F32)
    eC = sb("eC", [128, 1], F32)
    rbq = sb("rbq", [128, 1024], F32)
    rcq = sb("rcq", [128, 1024], F32)
    consts = {"Ccol": Ccol, "nC8": nC8, "nC4": nC4, "nC2": nC2,
              "eC": eC, "rb": rbq[:], "rc": rcq[:]}

    krr = sb("krr", [128, 6 * 1024], BF16)
    vrr = sb("vrr", [128, 6 * HV], BF16)
    vr4 = vrr[:].rearrange("p (s h v) -> p s h v", s=6, v=VST)

    x_t = sb("x_t", [128, 3 * 1024], F32)
    cos_t = sb("cos_t", [128, 2 * 512], F32)
    sin_t = sb("sin_t", [128, 2 * 512], F32)
    qk_t = sb("qk_t", [128, 2048], F32)
    qr_t = sb("qr_t", [128, 1024], F32)
    kr_t = sb("kr_t", [128, 1024], F32)
    sq = sb("sq", [128, 1024], F32)
    sq2 = sb("sq2", [128, 1024], F32)
    qbuf = sb("qbuf", [128, 2048], F32)
    xn_bf = sb("xn_bf", [128, 1024], BF16)
    xn_fm = sb("xn_fm", [128, 1024], BF16)
    qr_bf = sb("qr_bf", [128, 1024], BF16)
    kr_bf = sb("kr_bf", [128, 1024], BF16)
    q_fm = sb("q_fm", [128, 2048], BF16)
    ao_fm = sb("ao_fm", [128, 2048], BF16)
    xd_bf = sb("xd_bf", [128, 1024], BF16)
    xd_fm = sb("xd_fm", [128, 3072], BF16)
    pT_a = sb("pT_a", [128, 1280], BF16)
    pT_b = sb("pT_b", [128, 1280], BF16)
    mean = sb("mean", [128, 1], F32)
    var = sb("var", [128, 1], F32)
    msq = sb("msq", [128, 1], F32)
    rstd = sb("rstd", [128, 1], F32)
    rstd2 = sb("rstd2", [128, 1], F32)
    am = sb("am", [128, 32], F32)
    scale_t = sb("scale_t", [128, 32], F32)
    inv = sb("inv", [128, 32], F32)
    qn2 = sb("qn2", [128, 16], F32)
    kn2s = sb("kn2s", [128, 16], F32)
    qn2_bf = sb("qn2_bf", [128, 16], BF16)
    kn2_bf = sb("kn2_bf", [128, 16], BF16)
    knhist = sb("knhist", [16, 32], F32)
    w16 = sb("w16", [16, 1], F32)
    w16b = sb("w16b", [16, 1], F32)
    w16c = sb("w16c", [16, 1], F32)
    nm_bf = sb("nm_bf", [16, 1], BF16)
    nmrow = sb("nmrow", [1, 16], BF16)
    negm_bc = sb("negm_bc", [128, 32], F32)
    linv_all = sb("linv_all", [128, 1024], BF16)
    ones_all = sb("ones_all", [128, 64], BF16)
    fwq_bf = sb("fwq_bf", [128, 1024], BF16)

    nmean = sb("nmean", [128, 1], F32)
    lnb = {"mean": mean, "var": var, "msq": msq, "rstd": rstd,
           "rstd2": rstd2, "sq": sq, "nmean": nmean}
    qb = {"am": am, "scale": scale_t, "inv": inv,
          "y": qk_t[:, 0:1024], "c2": qk_t[:, 1024:2048], "c4": qr_t[:],
          "r1": kr_t[:], "r2": sq[:], "r3": qbuf[:, 1024:2048],
          "ra": qbuf[:, 0:1024]}

    ptA_ps = ps("ptA", [128, 512], F32)
    pt_tr = ptA_ps[:].bitcast(BF16)
    ptB_ps = ps("ptB", [128, 512], F32)
    ptB_tr = ptB_ps[:].bitcast(BF16)
    ppv = ptB_ps
    SC = ps("SCb", [128, 1536], F32)
    qv0 = ps("qv0", [128, 512], F32)
    qv1 = ps("qv1", [128, 512], F32)
    pq0 = ps("pq0b", [128, 512], F32)

    def prep_pool(g):
        c.PL.inc(g.memset(ones_row[:], 1.0))
        c.PL.inc(g.memset(Ccol[:], C_MAGIC))
        c.PL.inc(g.memset(nC8[:], -C_MAGIC / 8.0))
        c.PL.inc(g.memset(nC4[:], -C_MAGIC / 4.0))
        c.PL.inc(g.memset(nC2[:], -C_MAGIC / 2.0))
        c.PL.inc(g.memset(eC[:], 1e-30))
        c.PL.inc(g.memset(vr4[:, :, :, 64:65], 1.0))
        c.PL.inc(g.memset(ones_all[:], 1.0))
        for ch in range(8):
            c.GD.inc(g.dma_start(out=wqkv[:, 3072 * ch:3072 * ch + 3072],
                                 in_=w_qkv[128 * ch:128 * ch + 128, :]), 16)
        for ch in range(8):
            c.GD.inc(g.dma_start(out=wo[:, 1024 * ch:1024 * ch + 1024],
                                 in_=w_o[128 * ch:128 * ch + 128, :]), 16)
        c.GD.inc(g.dma_start(out=fbias[:], in_=fb), 16)
    c.gp(prep_pool)
    prep_gd = c.GD.n
    prep_pl = c.PL.n

    def prep_sync(e):
        c.LD.inc(e.dma_start(out=ident[:], in_=identc), 16)
        c.LD.inc(e.dma_start(out=mbO[:], in_=m_oldc), 16)
        c.LD.inc(e.dma_start(out=mbD[:], in_=m_diagc), 16)
        c.LD.inc(e.dma_start(out=x_t[:, 0:1024], in_=x_in[0:128, :]), 16)
        c.LD.inc(e.dma_start(out=cos_t[:, 0:512], in_=cosr[0:128, :]), 16)
        c.LD.inc(e.dma_start(out=sin_t[:, 0:512], in_=sinr[0:128, :]), 16)
        c.LD.inc(e.dma_start(out=x_t[:, 1024:2048], in_=x_in[128:256, :]), 16)
        c.LD.inc(e.dma_start(out=cos_t[:, 512:1024], in_=cosr[128:256, :]), 16)
        c.LD.inc(e.dma_start(out=sin_t[:, 512:1024], in_=sinr[128:256, :]), 16)
    c.sy(prep_sync)
    const_ld = c.LD.n

    def prep_if(sx, cl=const_ld):
        sx.wait_ge(c.LD.h, cl)
        c.ACT.inc(sx.activation(identF[:], ident[:], AF.Copy))
    c.sc(prep_if)

    def prep_mb(v, cl=const_ld):
        v.wait_ge(c.LD.h, cl)
        v.tensor_scalar(mbD[:], mbD[:], 1e30, -1e30, ALU.mult, ALU.add)
        v.drain()
        c.DVE.inc(v.tensor_scalar(mbO[:], mbO[:], 1e30, -1e30,
                                  ALU.mult, ALU.add))
        v.drain()
    c.ve(prep_mb)
    mb_ready = c.DVE.n

    fw_t = x_t[:, 2048:3072]
    for ch in range(8):
        prev_q = c.DVE.n

        def ld(e, ch=ch, prev_tr=c.PE.n, ppl=prep_pl, pcv=c.DVE.n):
            if ch == 0:
                e.wait_ge(c.PL.h, ppl)
            else:
                e.wait_ge(c.DVE.h, pcv)
            c.LDW.inc(e.dma_start(out=fw_t,
                                  in_=fw[128 * ch:128 * ch + 128, :]), 16)
        c.sy(ld)
        ld_done = c.LDW.n

        def cvt(v, ld_done=ld_done, prev_tr=c.PE.n):
            v.wait_ge(c.LDW.h, ld_done)
            v.wait_ge(c.PE.h, prev_tr)
            c.DVE.inc(v.tensor_copy(fwq_bf[:], fw_t))
        c.ve(cvt)
        q_done = emit_quant_fp6(c, fwq_bf, fwq_bf, qb, consts, c.DVE.n)

        def ten(t, q_done=q_done, prev_ev=c.ACT.n, cl=const_ld):
            t.wait_ge(c.DVE.h, q_done)
            t.wait_ge(c.ACT.h, prev_ev)
            t.wait_ge(c.LD.h, cl)
            for d in range(8):
                i = t.transpose(pt_tr[:, 128 * d:128 * d + 128],
                                fwq_bf[:, 128 * d:128 * d + 128], ident[:])
            c.PE.inc(i)
        c.te(ten)
        tr_done = c.PE.n

        def ev(s, ch=ch, tr_done=tr_done):
            s.wait_ge(c.PE.h, tr_done)
            pv = pt_tr[:].rearrange("p (d k) -> p d k", k=128)
            dv = wdq[:].rearrange("p (d o) -> p d o", o=1024)
            c.ACT.inc(s.activation(dv[:, :, 128 * ch:128 * ch + 128], pv,
                                   AF.Copy))
        c.sc(ev)
    wq_done_act = c.ACT.n
    fw_lds = c.LD.n
    fw_prep_dve = c.DVE.n

    joins = c.marks()
    st_counts = {}
    ld_counts = {}
    am = {}
    bm = {}

    def emit_pre(tp):
        def pre(e, tp=tp, ldn=c.LD.n,
                qro=(am[tp]["qrope"], am[tp]["krope"]),
                stw=(st_counts.get(tp - 1)), fpd=fw_prep_dve):
            if tp + 2 < NT:
                e.wait_ge(c.LD.h, ldn)
                if tp == 0:
                    e.wait_ge(c.DVE.h, fpd)
                e.wait_ge(c.DVE.h, qro[0])
                e.wait_ge(c.PL.h, qro[1])
                if stw is not None:
                    e.wait_ge(c.GDS.h, stw)
                nb3 = (tp + 2) % 3
                nb = (tp + 2) % 2
                r0 = 128 * (tp + 2)
                c.LD.inc(e.dma_start(
                    out=x_t[:, 1024 * nb3:1024 * nb3 + 1024],
                    in_=x_in[r0:r0 + 128, :]), 16)
                c.LD.inc(e.dma_start(
                    out=cos_t[:, 512 * nb:512 * nb + 512],
                    in_=cosr[r0:r0 + 128, :]), 16)
                c.LD.inc(e.dma_start(
                    out=sin_t[:, 512 * nb:512 * nb + 512],
                    in_=sinr[r0:r0 + 128, :]), 16)
        c.sy(pre)

    lnm = {}

    def emit_LN(t):
        buf3 = t % 3
        ld_counts[t] = c.LD.n
        pa = am.get(t - 1)
        ln_vw = [(c.PE, pa["xn_tr"])] if pa else ()
        ln_sw = [(c.DVE, pa["qadd"])] if pa else ()
        lnm[t] = emit_layernorm(
            c, x_t[:, 1024 * buf3:1024 * buf3 + 1024],
            xn_bf, lnb, joins if t == 0 else None,
            extra_ld=ld_counts[t],
            vec_waits=ln_vw, sca_waits=ln_sw)

    def emit_A(t):
        buf3 = t % 3
        buf = t % 2
        slot = t % 6
        pa = am.get(t - 1)
        pb = bm.get(t - 2)
        xn_ready = lnm[t]

        pe_w = [(c.DVE, xn_ready)]
        if t == 0:
            pe_w.append((c.ACT, wq_done_act))
        else:
            pe_w.append((c.ACT, pa["xdfm"]))
        xnfm_ready = emit_transpose8(
            c, xn_bf, pt_tr, ident,
            lambda: (xn_fm[:], pt_tr[:]), pe_wait=pe_w)
        xn_tr_pe = c.PE.n

        evac_marks = []
        for g in range(6):
            bank = qv0[:] if g % 2 == 0 else qv1[:]

            def mm(tn, g=g, bank=bank, xnfm_ready=xnfm_ready, t=t,
                   pj=(None if t > 0 else joins["DVE"]),
                   pjp=(pa["em5"] if pa else None),
                   pnb=(pa["nbc"] if pa else None),
                   em2=(evac_marks[g - 2] if g >= 2 else None)):
                if g == 0:
                    tn.wait_ge(c.ACT.h, xnfm_ready)
                    if t == 0:
                        tn.wait_ge(c.GD.h, prep_gd)
                        tn.wait_ge(c.DVE.h, pj)
                    else:
                        tn.wait_ge(pjp[0].h, pjp[1])
                        tn.wait_ge(c.ACT.h, pnb)
                if em2 is not None:
                    tn.wait_ge(em2[0].h, em2[1])
                for ch in range(8):
                    i = tn.matmul(
                        bank, xn_fm[:, 128 * ch:128 * ch + 128],
                        wqkv[:, 3072 * ch + 512 * g:3072 * ch + 512 * g + 512],
                        start=(ch == 0), stop=(ch == 7))
                c.PE.inc(i)
            c.te(mm)
            mm_done = c.PE.n

            if g < 4:
                def ev(sx, g=g, bank=bank, mm_done=mm_done):
                    sx.wait_ge(c.PE.h, mm_done)
                    c.ACT.inc(sx.activation(qk_t[:, 512 * g:512 * g + 512],
                                            bank, AF.Copy))
                c.sc(ev)
                evac_marks.append((c.ACT, c.ACT.n))
            else:
                def ev(v, g=g, bank=bank, mm_done=mm_done, slot=slot):
                    v.wait_ge(c.PE.h, mm_done)
                    pv3 = bank.rearrange("p (h k) -> p h k", k=64)
                    hh = 0 if g == 4 else 8
                    c.DVE.inc(v.tensor_copy(vr4[:, slot, hh:hh + 8, 0:64],
                                            pv3))
                c.ve(ev)
                evac_marks.append((c.DVE, c.DVE.n))
        qk_ready_act = evac_marks[3][1]

        qk4 = qk_t[:, 0:1024].rearrange("p (h z f) -> p h z f", z=2, f=32)
        kk4 = qk_t[:, 1024:2048].rearrange("p (h z f) -> p h z f", z=2, f=32)
        qr4 = qr_t[:].rearrange("p (h z f) -> p h z f", z=2, f=32)
        kr4 = kr_t[:].rearrange("p (h z f) -> p h z f", z=2, f=32)

        def rope_v(v, buf=buf, t=t, qka=qk_ready_act):
            v.wait_ge(c.ACT.h, qka)
            a = qk4[:, :, 0, :]
            bq = qk4[:, :, 1, :]
            oe = qr4[:, :, 0, :]
            oo = qr4[:, :, 1, :]
            cosv = cos_t[:, 512 * buf:512 * buf + 512].rearrange(
                "p (h f) -> p h f", f=32)
            sinv = sin_t[:, 512 * buf:512 * buf + 512].rearrange(
                "p (h f) -> p h f", f=32)
            s2a = sq2[:, 0:512].rearrange("p (h f) -> p h f", f=32)
            s2b = sq2[:, 512:1024].rearrange("p (h f) -> p h f", f=32)
            s2c = qbuf[:, 1024:1536].rearrange("p (h f) -> p h f", f=32)
            s2d = qbuf[:, 1536:2048].rearrange("p (h f) -> p h f", f=32)
            v.tensor_tensor(s2a, a, cosv, ALU.mult)
            v.tensor_tensor(s2b, bq, sinv, ALU.mult)
            v.tensor_tensor(s2c, bq, cosv, ALU.mult)
            c.DVE.inc(v.tensor_tensor(s2d, a, sinv, ALU.mult))
            v.drain()
            v.tensor_sub(oe, s2a, s2b)
            c.DVE.inc(v.tensor_add(oo, s2c, s2d))
        c.ve(rope_v)
        qrope_done = c.DVE.n

        def rope_p(g, buf=buf, qk_ready=qk_ready_act,
                   prev=(joins if t == 0 else None),
                   pnorm=(am[t - 1]["norms"] if t >= 1 else None)):
            wait_marks(c, g, prev)
            g.wait_ge(c.ACT.h, qk_ready)
            if pnorm is not None:
                g.wait_ge(c.DVE.h, pnorm)
            a = kk4[:, :, 0, :]
            bk = kk4[:, :, 1, :]
            oe = kr4[:, :, 0, :]
            oo = kr4[:, :, 1, :]
            cosv = cos_t[:, 512 * buf:512 * buf + 512].rearrange(
                "p (h f) -> p h f", f=32)
            sinv = sin_t[:, 512 * buf:512 * buf + 512].rearrange(
                "p (h f) -> p h f", f=32)
            qba = qbuf[:, 0:512].rearrange("p (h f) -> p h f", f=32)
            qbb = qbuf[:, 512:1024].rearrange("p (h f) -> p h f", f=32)
            sqa = sq[:, 0:512].rearrange("p (h f) -> p h f", f=32)
            sqb = sq[:, 512:1024].rearrange("p (h f) -> p h f", f=32)
            g.tensor_tensor(qba, a, cosv, ALU.mult)
            g.tensor_tensor(qbb, bk, sinv, ALU.mult)
            g.tensor_tensor(sqa, bk, cosv, ALU.mult)
            c.PL.inc(g.tensor_tensor(sqb, a, sinv, ALU.mult))
            g.drain()
            g.tensor_sub(oe, qba, qbb)
            c.PL.inc(g.tensor_add(oo, sqa, sqb))
        c.gp(rope_p)
        krope_done = c.PL.n

        def casts(s, qrope_done=qrope_done, krope_done=krope_done):
            s.wait_ge(c.DVE.h, qrope_done)
            c.ACT.inc(s.activation(qr_bf[:], qr_t[:], AF.Copy, scale=0.125))
            s.wait_ge(c.PL.h, krope_done)
            c.ACT.inc(s.activation(kr_bf[:], kr_t[:], AF.Copy))
        c.sc(casts)
        qsq_done = c.ACT.n - 1
        ksq_done = c.ACT.n

        def sqs(g, qrope_done=qrope_done):
            g.wait_ge(c.DVE.h, qrope_done)
            g.tensor_tensor(sq[:], qr_t[:], qr_t[:], ALU.mult)
            c.PL.inc(g.tensor_tensor(sq2[:], kr_t[:], kr_t[:], ALU.mult))
        c.gp(sqs)
        sq_pl = c.PL.n

        def norms(v, sq_pl=sq_pl, t=t):
            v.wait_ge(c.PL.h, sq_pl)
            s3 = sq[:].rearrange("p (h d) -> p h d", d=64)
            c.DVE.inc(v.tensor_reduce(qn2[:], s3, AX.X, ALU.add))
            s32 = sq2[:].rearrange("p (h d) -> p h d", d=64)
            c.DVE.inc(v.tensor_reduce(kn2s[:], s32, AX.X, ALU.add))
            v.drain()
            v.tensor_copy(qn2_bf[:], qn2[:])
            c.DVE.inc(v.tensor_copy(kn2_bf[:], kn2s[:]))
        c.ve(norms)
        norms_done = c.DVE.n

        def tn_qn(tn, norms_done=norms_done, xnfm_ready=xnfm_ready):
            tn.wait_ge(c.DVE.h, norms_done)
            tn.wait_ge(c.ACT.h, xnfm_ready)
            c.PE.inc(tn.transpose(pt_tr[0:16, 0:128], qn2_bf[:], ident[:]))
            c.PE.inc(tn.transpose(pt_tr[0:16, 128:256], kn2_bf[:], ident[:]))
        c.te(tn_qn)
        tqn_done = c.PE.n

        def redn(v, tqn_done=tqn_done, t=t):
            v.wait_ge(c.PE.h, tqn_done)
            c.DVE.inc(v.tensor_reduce(w16[:], pt_tr[0:16, 0:128], AX.X,
                                      ALU.max))
            c.DVE.inc(v.tensor_reduce(knhist[:, t:t + 1],
                                      pt_tr[0:16, 128:256], AX.X, ALU.max))
            v.drain()
            lo = max(0, t - 4)
            c.DVE.inc(v.tensor_reduce(w16b[:], knhist[:, lo:t + 1], AX.X,
                                      ALU.max))
            v.drain()
            c.DVE.inc(v.scalar_tensor_tensor(w16[:], w16[:], 1.0,
                                             w16b[:], ALU.mult, ALU.add))
        c.ve(redn)
        nbc_done = emit_negm_bcast(
            c, w16, w16c, nm_bf, nmrow, pt_tr,
            qv0[:], negm_bc[:, 16 * buf:16 * buf + 16], ident,
            ones_row, 1.0 / 16.0, c.DVE.n,
            bcm_waits=[(c.DVE, evac_marks[4][1])],
            af=AF.Copy)

        qfm_ready = emit_transpose8(
            c, qr_bf, pt_tr, ident,
            lambda buf=buf: (q_fm[:, 1024 * buf:1024 * buf + 1024], pt_tr[:]),
            pe_wait=[(c.ACT, qsq_done), (c.ACT, nbc_done)])
        kfm_ready = emit_transpose8(
            c, kr_bf, pt_tr, ident,
            lambda slot=slot: (krr[:, 1024 * slot:1024 * slot + 1024],
                               pt_tr[:]),
            pe_wait=[(c.ACT, qfm_ready)])

        xd_done = emit_quant_fp6(c, xn_bf, xd_bf, qb, consts, norms_done)
        xdfm_ready = emit_transpose8(
            c, xd_bf, pt_tr, ident,
            lambda x3=(t % 3): (xd_fm[:, 1024 * x3:1024 * x3 + 1024],
                                pt_tr[:]),
            pe_wait=[(c.DVE, xd_done), (c.ACT, kfm_ready)],
            evac_wait=([(c.PE, bm[t - 3]["fm1"])] if t >= 3 else None))

        am[t] = {"qadd": xd_done, "xn_tr": xn_tr_pe, "xdfm": xdfm_ready,
                 "kfm": kfm_ready, "qfm": qfm_ready, "nbc": nbc_done,
                 "em5": evac_marks[5], "qrope": qrope_done,
                 "krope": krope_done, "norms": norms_done}

    def emit_B1(t, next_nbc):
        buf3 = t % 3
        buf = t % 2
        a_t = am[t]
        pb = bm.get(t - 1)
        lo_tile = max(0, t - 4)
        nsl = t - lo_tile + 1
        qfm_s = q_fm[:, 1024 * buf:1024 * buf + 1024]
        nbc_s = negm_bc[:, 16 * buf:16 * buf + 16]
        exp_hist = []
        tail_hist = []
        pv_pe = []
        pend0 = None

        def mk_pvt(hp, nsl, lo_tile, exp_done, po, pr):
            def pv(tn, hp=hp, nsl=nsl, lo_tile=lo_tile, exp_done=exp_done,
                   po=po, pr=pr,
                   prev_tail=(tail_hist[hp - 1] if hp >= 1 else None),
                   xdp=(pb["tail7"] if (hp == 0 and pb) else None)):
                tn.wait_ge(c.ACT.h, exp_done)
                if prev_tail is not None:
                    tn.wait_ge(c.ACT.h, prev_tail)
                if xdp is not None:
                    tn.wait_ge(c.ACT.h, xdp)
                for j, (hd, pT) in enumerate(((2 * hp, pT_a),
                                              (2 * hp + 1, pT_b))):
                    for si in range(nsl):
                        sl = ((lo_tile + si) % 6)
                        v0 = HV * sl + VST * hd
                        i = tn.matmul(
                            ppv[0:65, pr + 128 * j:pr + 128 * j + 128],
                            vrr[:, v0:v0 + VST],
                            pT[:, po + 128 * si:po + 128 * si + 128],
                            start=(si == 0), stop=(si == nsl - 1))
                c.PE.inc(i)
            c.te(pv)
            pv_done = c.PE.n
            pv_pe.append(pv_done)

            def tailc(sx, hp=hp, pr=pr, pv_done=pv_done, buf=buf,
                      pw=((bm[t - 2]["wom2"] if (t >= 2 and hp == 0)
                           else None))):
                sx.wait_ge(c.PE.h, pv_done)
                if pw is not None:
                    sx.wait_ge(c.PE.h, pw)
                ao = 1024 * buf
                sx.activation(ao_fm[0:64, ao + 128 * hp:ao + 128 * hp + 128],
                              ppv[0:64, pr:pr + 128], AF.Copy)
                sx.activation(
                    ao_fm[64:128, ao + 128 * hp:ao + 128 * hp + 128],
                    ppv[0:64, pr + 128:pr + 256], AF.Copy)
                sx.activation(linv_all[0:1, 128 * hp:128 * hp + 128],
                              ppv[64:65, pr:pr + 128], AF.Copy)
                c.ACT.inc(sx.activation(
                    linv_all[64:65, 128 * hp:128 * hp + 128],
                    ppv[64:65, pr + 128:pr + 256], AF.Copy))
            c.sc(tailc)
            tail_hist.append(c.ACT.n)

        for hp in range(8):
            if hp >= 1:
                sc_prev = exp_hist[hp - 1]
            elif pb:
                sc_prev = pb["exps7"]
            else:
                sc_prev = None
            sc_dve = pb["ao_all"] if (hp == 0 and pb) else None

            def smm(tn, hp=hp, lo_tile=lo_tile, nsl=nsl, t=t,
                    kfm_ready=a_t["kfm"], prev=sc_prev, scd=sc_dve,
                    qfm_s=qfm_s,
                    mbw=(mb_ready if (t == 0 and hp == 0) else None)):
                tn.wait_ge(c.ACT.h, kfm_ready)
                if prev is not None:
                    tn.wait_ge(c.ACT.h, prev)
                if scd is not None:
                    tn.wait_ge(c.DVE.h, scd)
                if mbw is not None:
                    tn.wait_ge(c.DVE.h, mbw)
                for si in range(nsl):
                    sl = ((lo_tile + si) % 6)
                    k0 = 1024 * sl + 128 * hp
                    mb = (mbD if si == nsl - 1 else
                          (mbO if (si == 0 and t >= 4) else None))
                    i = tn.matmul(SC[:, 128 * si:128 * si + 128],
                                  krr[0:64, k0:k0 + 128],
                                  qfm_s[0:64, 128 * hp:128 * hp + 128],
                                  start=True, stop=(mb is None))
                    i = tn.matmul(SC[:, 768 + 128 * si:768 + 128 * si + 128],
                                  krr[64:128, k0:k0 + 128],
                                  qfm_s[64:128, 128 * hp:128 * hp + 128],
                                  start=True, stop=(mb is None),
                                  tile_position=(64, 0))
                    if mb is not None:
                        i = tn.matmul(SC[:, 128 * si:128 * si + 128],
                                      ident[:], mb[:],
                                      start=False, stop=True)
                        i = tn.matmul(
                            SC[:, 768 + 128 * si:768 + 128 * si + 128],
                            ident[:], mb[:], start=False, stop=True)
                c.PE.inc(i)
            c.te(smm)
            smm_done = c.PE.n

            po = 640 * (hp % 2)

            def ex(sx, hp=hp, nsl=nsl, smm_done=smm_done, po=po,
                   nbc_s=nbc_s,
                   pvprev=(pv_pe[hp - 2] if hp >= 2 else
                           (pb["pv6"] if (hp == 0 and pb) else
                            (pb["pv7"] if (hp == 1 and pb) else None)))):
                sx.wait_ge(c.PE.h, smm_done)
                if pvprev is not None:
                    sx.wait_ge(c.PE.h, pvprev)
                c.ACT.inc(sx.activation(pT_a[:, po:po + 128 * nsl],
                                        SC[:, 0:128 * nsl], AF.Exp,
                                        bias=nbc_s[:, 2 * hp:2 * hp + 1]))
                c.ACT.inc(sx.activation(
                    pT_b[:, po:po + 128 * nsl],
                    SC[:, 768:768 + 128 * nsl], AF.Exp,
                    bias=nbc_s[:, 2 * hp + 1:2 * hp + 2]))
            c.sc(ex)
            exp_hist.append(c.ACT.n)

            pr = 256 * (hp % 2)
            if pend0 is not None:
                mk_pvt(*pend0)
            pend0 = (hp, nsl, lo_tile, exp_hist[hp], po, pr)
        mk_pvt(*pend0)

        def bcast(tn, tail7=tail_hist[7], e7=exp_hist[7]):
            tn.wait_ge(c.ACT.h, tail7)
            tn.wait_ge(c.ACT.h, e7)
            for g2 in range(2):
                i = tn.matmul(SC[0:64, 512 * g2:512 * g2 + 512],
                              ones_all[0:1, 0:64],
                              linv_all[0:1, 512 * g2:512 * g2 + 512],
                              start=True, stop=True)
                i = tn.matmul(SC[64:128, 512 * g2:512 * g2 + 512],
                              ones_all[64:65, 0:64],
                              linv_all[64:65, 512 * g2:512 * g2 + 512],
                              start=True, stop=True, tile_position=(64, 64))
            c.PE.inc(i)
        c.te(bcast)
        bcast_done = c.PE.n

        def nrm(v, bcast_done=bcast_done, tail7=tail_hist[7], buf=buf):
            v.wait_ge(c.PE.h, bcast_done)
            v.wait_ge(c.ACT.h, tail7)
            c.DVE.inc(v.reciprocal(sc_bf[:], SC[:, 0:1024]))
            v.drain()
            ao = 1024 * buf
            c.DVE.inc(v.tensor_tensor(ao_fm[:, ao:ao + 1024],
                                      ao_fm[:, ao:ao + 1024], sc_bf[:],
                                      ALU.mult))
        c.ve(nrm)
        ao_all = c.DVE.n

        bm[t] = {"exps7": exp_hist[7], "pv6": pv_pe[6], "pv7": pv_pe[7],
                 "tail7": tail_hist[7], "ao_all": ao_all,
                 "smm7": smm_done}

    def emit_B2(t):
        buf3 = t % 3
        x3 = t % 3
        a_t = am[t]
        ao_all = bm[t]["ao_all"]

        x1ev_marks = []
        for g in range(2):
            def wff(tn, g=g, ao_all=ao_all, xdfm_ready=a_t["xdfm"],
                    x3=x3, buf3=buf3,
                    pfree=(bm[t - 1]["x1_done"] if t >= 1 else None)):
                if g == 0:
                    tn.wait_ge(c.DVE.h, ao_all)
                    tn.wait_ge(c.ACT.h, xdfm_ready)
                    if pfree is not None:
                        tn.wait_ge(c.ACT.h, pfree)
                else:
                    tn.wait_ge(c.ACT.h, x1ev_marks[0])
                tn.matmul(pq0[:], identF[:],
                          x_t[:, 1024 * buf3 + 512 * g:
                              1024 * buf3 + 512 * g + 512],
                          start=True, stop=False)
                ao = 1024 * (t % 2)
                for ch in range(8):
                    tn.matmul(
                        pq0[:], ao_fm[:, ao + 128 * ch:ao + 128 * ch + 128],
                        wo[:, 1024 * ch + 512 * g:1024 * ch + 512 * g + 512],
                        start=False, stop=False)
                for ch in range(8):
                    tn.matmul(
                        pq0[:],
                        xd_fm[:, 1024 * x3 + 128 * ch:
                              1024 * x3 + 128 * ch + 128],
                        wdq[:, 1024 * ch + 512 * g:1024 * ch + 512 * g + 512],
                        start=False, stop=False)
                i = tn.matmul(pq0[:], ones_row[0:1, 0:128],
                              fbias[0:1, 512 * g:512 * g + 512],
                              start=False, stop=True)
                c.PE.inc(i)
            c.te(wff)
            wff_done = c.PE.n
            if g == 1:
                wom_mark2 = wff_done

            def x1ev(sx, g=g, buf3=buf3, wff_done=wff_done):
                sx.wait_ge(c.PE.h, wff_done)
                c.ACT.inc(sx.activation(
                    x_t[:, 1024 * buf3 + 512 * g:
                        1024 * buf3 + 512 * g + 512],
                    pq0[:, 0:512], AF.Copy))
            c.sc(x1ev)
            x1ev_marks.append(c.ACT.n)
        x1_done = x1ev_marks[1]
        fm_done = wom_mark2

        def st(g, t=t, buf3=buf3, x1_done=x1_done, gsn=c.GDS.n):
            g.wait_ge(c.GDS.h, gsn)
            g.wait_ge(c.ACT.h, x1_done)
            c.GDS.inc(g.dma_start(
                out=x1[128 * t:128 * t + 128, :],
                in_=x_t[:, 1024 * buf3:1024 * buf3 + 1024]), 16)
        c.gp(st)
        st_counts[t] = c.GDS.n

        bm[t]["x1_done"] = x1_done
        bm[t]["fm1"] = fm_done
        bm[t]["wom2"] = wom_mark2
        if t + 1 < NT:
            emit_pre(t + 1)

    emit_LN(0)
    emit_A(0)
    emit_LN(1)
    emit_A(1)
    emit_pre(0)
    emit_B1(0, next_nbc=None)
    for r in range(2, NT):
        emit_LN(r)
        emit_B1(r - 1, next_nbc=None)
        emit_A(r)
        emit_B2(r - 2)
    emit_B1(NT - 1, next_nbc=None)
    emit_B2(NT - 2)
    emit_B2(NT - 1)
    st_counts = [st_counts[t] for t in range(NT)]

    return {"x1_store_gd": st_counts, "end": c.marks()}



def perm_qkv_cols(w):
    idx = np.arange(3 * D)
    for part in range(2):
        for h in range(H):
            base = 1024 * part + 64 * h
            idx[base:base + 32] = base + 2 * np.arange(32)
            idx[base + 32:base + 64] = base + 2 * np.arange(32) + 1
    return np.ascontiguousarray(w[:, idx])


def host_constants():
    ident = np.eye(128, dtype=np.float32).astype(ml_dtypes.bfloat16)
    pos = np.arange(S, dtype=np.float64)
    invf = (1.0 / (10000.0 ** (np.arange(32).astype(np.float32) / 32.0))
            ).astype(np.float32)
    ang = (pos[:, None] * invf[None, :].astype(np.float64))
    ang32 = (pos.astype(np.float32)[:, None] * invf[None, :]).astype(np.float64)
    cos = np.cos(ang32).astype(np.float32)
    sin = np.sin(ang32).astype(np.float32)
    cosr = np.tile(cos, (1, 16)).astype(np.float32)
    sinr = np.tile(sin, (1, 16)).astype(np.float32)
    r = np.arange(128)
    m_old = (r[:, None] >= r[None, :]).astype(np.float32)
    m_diag = (r[:, None] <= r[None, :]).astype(np.float32)
    m_old = m_old.astype(ml_dtypes.bfloat16)
    m_diag = m_diag.astype(ml_dtypes.bfloat16)
    ind16 = np.zeros((128, 128), np.float32)
    for h in range(16):
        c0 = h // 2
        rr0 = 64 * (h % 2)
        ind16[rr0:rr0 + 64, 16 * c0 + h] = 1.0
    ind16 = ind16.astype(ml_dtypes.bfloat16)
    return {"ident": ident, "cosr": cosr, "sinr": sinr, "m_old": m_old,
            "m_diag": m_diag, "ind16": ind16}


def build_l0_program(x1_kind="ExternalOutput"):
    nc = bass.Bass("TRN2", target_bir_lowering=False, debug=BASS_DEBUG)
    io = {}
    io["x"] = nc.dram_tensor("x", [S, D], F32, kind="ExternalInput").ap()
    io["w_qkv"] = nc.dram_tensor("w_qkv", [D, 3 * D], F32,
                                 kind="ExternalInput").ap()
    io["w_o"] = nc.dram_tensor("w_o", [D, D], F32, kind="ExternalInput").ap()
    io["fw"] = nc.dram_tensor("fw", [D, D], F32, kind="ExternalInput").ap()
    io["fb"] = nc.dram_tensor("fb", [1, D], F32, kind="ExternalInput").ap()
    io["cosr"] = nc.dram_tensor("cosr", [S, 512], F32,
                                kind="ExternalInput").ap()
    io["sinr"] = nc.dram_tensor("sinr", [S, 512], F32,
                                kind="ExternalInput").ap()
    io["ident"] = nc.dram_tensor("ident", [128, 128], BF16,
                                 kind="ExternalInput").ap()
    io["m_old"] = nc.dram_tensor("m_old", [128, 128], BF16,
                                 kind="ExternalInput").ap()
    io["m_diag"] = nc.dram_tensor("m_diag", [128, 128], BF16,
                                  kind="ExternalInput").ap()
    io["x1"] = nc.dram_tensor("x1", [S, D], F32, kind=x1_kind).ap()
    if DEBUG_TILE >= 0:
        io["dbg"] = {
            "d_xn": nc.dram_tensor("d_xn", [128, 1024], BF16, kind="ExternalOutput").ap(),
            "d_qk": nc.dram_tensor("d_qk", [128, 2048], F32, kind="ExternalOutput").ap(),
            "d_qr": nc.dram_tensor("d_qr", [128, 1024], F32, kind="ExternalOutput").ap(),
            "d_kr": nc.dram_tensor("d_kr", [128, 1024], F32, kind="ExternalOutput").ap(),
            "d_negm": nc.dram_tensor("d_negm", [128, 16], F32, kind="ExternalOutput").ap(),
            "d_pta": nc.dram_tensor("d_pta", [128, 640], BF16, kind="ExternalOutput").ap(),
            "d_ao": nc.dram_tensor("d_ao", [128, 1024], BF16, kind="ExternalOutput").ap(),
            "d_xd": nc.dram_tensor("d_xd", [128, 1024], BF16, kind="ExternalOutput").ap(),
            "d_qfm": nc.dram_tensor("d_qfm", [128, 1024], BF16, kind="ExternalOutput").ap(),
            "d_krr": nc.dram_tensor("d_krr", [128, 1024], BF16, kind="ExternalOutput").ap(),
            "d_vrr": nc.dram_tensor("d_vrr", [128, 1040], BF16, kind="ExternalOutput").ap(),
            "d_mean": nc.dram_tensor("d_mean", [128, 1], F32, kind="ExternalOutput").ap(),
            "d_var": nc.dram_tensor("d_var", [128, 1], F32, kind="ExternalOutput").ap(),
            "d_rstd": nc.dram_tensor("d_rstd", [128, 1], F32, kind="ExternalOutput").ap(),
            "d_msq": nc.dram_tensor("d_msq", [128, 1], F32, kind="ExternalOutput").ap(),
            "d_xt": nc.dram_tensor("d_xt", [128, 1024], F32, kind="ExternalOutput").ap(),
            "d_qn2": nc.dram_tensor("d_qn2", [128, 16], BF16, kind="ExternalOutput").ap(),
            "d_kn2": nc.dram_tensor("d_kn2", [128, 16], BF16, kind="ExternalOutput").ap(),
            "d_knh": nc.dram_tensor("d_knh", [16, 32], F32, kind="ExternalOutput").ap(),
            "d_w16": nc.dram_tensor("d_w16", [16, 1], F32, kind="ExternalOutput").ap(),
            "d_w16b": nc.dram_tensor("d_w16b", [16, 1], F32, kind="ExternalOutput").ap(),
            "d_ptqn": nc.dram_tensor("d_ptqn", [16, 256], F32, kind="ExternalOutput").ap(),
        }
    es = ExitStack()
    with es:
        with nc.Block() as block:
            c = Ctx(nc, block, es)
            marks = build_l0(nc, c, io, es)
            def fin(e, n=c.GDS.n, n2=c.GD.n):
                e.wait_ge(c.GDS.h, n)
                e.wait_ge(c.GD.h, n2)
            c.sy(fin)
            c.flush()
    return nc



def build_l1(nc, c, io, es, x1_gd=None, joins0=None):
    if DEBUG_L1:
        io["d3_qfm"] = nc.dram_tensor("d3_qfm", [128, 4096], BF16,
                                      kind="ExternalOutput").ap()
        io["d3_negm"] = nc.dram_tensor("d3_negm", [128, 16], F32,
                                       kind="ExternalOutput").ap()
        io["d3_acc"] = nc.dram_tensor("d3_acc", [128, 4096], BF16,
                                      kind="ExternalOutput").ap()
        io["d3_kmax"] = nc.dram_tensor("d3_kmax", [16, 1], F32,
                                       kind="ExternalOutput").ap()
        io["d3_kfm"] = nc.dram_tensor("d3_kfm", [128, 1024], BF16,
                                      kind="ExternalOutput").ap()
        io["d3_vt"] = nc.dram_tensor("d3_vt", [128, 1040], BF16,
                                     kind="ExternalOutput").ap()
    x1 = io["x1"]
    wqkv1 = io["wqkv1"]
    wg, wu, wdn = io["wg"], io["wu"], io["wdn"]
    identc, ind16c = io["ident"], io["ind16"]
    out = io["out"]
    xnfm_s = io["xnfm_s"]
    x2p = io["x2p"]

    sb = lambda n, sh, dt: es.enter_context(nc.sbuf_tensor(n, sh, dt))

    ident = sb("ident1", [128, 128], BF16)
    ind16 = sb("ind16s", [128, 128], BF16)
    ones_row = sb("ones1", [1, 128], BF16)

    def prep(e):
        if joins0 is not None:
            wait_marks(c, e, joins0)
        if x1_gd is not None:
            e.wait_ge(c.GDS.h, x1_gd[-1])
        c.LD.inc(e.dma_start(out=ident[:], in_=identc), 16)
        c.LD.inc(e.dma_start(out=ind16[:], in_=ind16c), 16)
    c.sy(prep)

    def prep2(g):
        if joins0 is not None:
            wait_marks(c, g, joins0)
        if x1_gd is not None:
            g.wait_ge(c.GDS.h, x1_gd[-1])
        c.PL.inc(g.memset(ones_row[:], 1.0))
    c.gp(prep2)
    prep_ld1 = c.LD.n
    joins = c.marks()

    es23 = ExitStack()
    sb23 = lambda n, sh, dt: es23.enter_context(nc.sbuf_tensor(n, sh, dt))
    kfm = sb23("kfm", [128, 8 * S], BF16)
    vtm = sb23("vtm", [128, NT * HV], BF16)
    v4 = vtm[:].rearrange("p (s h v) -> p s h v", s=NT, v=VST)
    kmax2 = sb23("kmax2", [16, 1], F32)

    def prep_ones(g):
        if joins0 is not None:
            wait_marks(c, g, joins0)
        c.PL.inc(g.memset(v4[:, :, :, 64:65], 1.0))
    c.gp(prep_ones)
    joins = c.marks()

    es2 = ExitStack()
    sb2 = lambda n, sh, dt: es2.enter_context(nc.sbuf_tensor(n, sh, dt))
    ps2 = lambda n, sh, dt: es2.enter_context(nc.psum_tensor(n, sh, dt))

    wkvT = sb2("wkvT", [128, 16 * 1024], BF16)
    wtmp = sb2("wtmp", [128, 1024], BF16)
    x1_t = sb2("x1_t", [128, 2048], F32)
    sqP = sb2("sqP", [128, 1024], F32)
    meanP = sb2("meanP", [128, 1], F32)
    varP = sb2("varP", [128, 1], F32)
    msqP = sb2("msqP", [128, 1], F32)
    rstdP = sb2("rstdP", [128, 1], F32)
    rstd2P = sb2("rstd2P", [128, 1], F32)
    xn_bfP = sb2("xn_bfP", [128, 1024], BF16)
    xnfm4 = sb2("xnfm4", [128, 4096], BF16)
    sqk = sb2("sqk", [128, 1024], BF16)
    kn16 = sb2("kn16", [16, 1], F32)

    pt_tr = ps2("pt_tr1", [128, 1024], BF16)
    pk0 = ps2("pk0", [128, 512], F32)
    pk1 = ps2("pk1", [128, 512], F32)
    pn = ps2("pn", [16, 512], F32)

    nmeanP = sb2("nmeanP", [128, 1], F32)
    lnbP = {"mean": meanP, "var": varP, "msq": msqP, "rstd": rstdP,
            "rstd2": rstd2P, "sq": sqP, "nmean": nmeanP}

    wkv3 = wkvT[:].rearrange("p (part cd k) -> p part cd k", part=2, k=1024)
    prev_tr_pe = c.PE.n
    prev_ev_act = c.ACT.n
    for m in range(16):
        def ld(g, m=m, ptp=prev_tr_pe, joins=joins):
            wait_marks(c, g, joins if m == 0 else None)
            if m >= 1:
                g.wait_ge(c.PE.h, ptp)
            c.GD.inc(g.dma_start(
                out=wtmp[:],
                in_=wqkv1[1024 + 128 * m:1024 + 128 * m + 128, :]), 16)
        c.gp(ld)
        ld_done = c.GD.n

        def tr(tn, ld_done=ld_done, prev_ev_act=prev_ev_act, joins=joins,
               m=m):
            wait_marks(c, tn, joins if m == 0 else None)
            if m == 0:
                tn.wait_ge(c.LD.h, prep_ld1)
            tn.wait_ge(c.GD.h, ld_done)
            if m >= 1:
                tn.wait_ge(c.ACT.h, prev_ev_act)
            for ch in range(8):
                i = tn.transpose(pt_tr[:, 128 * ch:128 * ch + 128],
                                 wtmp[:, 128 * ch:128 * ch + 128], ident[:])
            c.PE.inc(i)
        c.te(tr)
        prev_tr_pe = c.PE.n

        def ev(sx, m=m, tr_done=c.PE.n):
            sx.wait_ge(c.PE.h, tr_done)
            pv = pt_tr[:].rearrange("p (ch k) -> p ch k", k=128)
            part = 0 if m < 8 else 1
            c.ACT.inc(sx.activation(
                wkv3[:, part, :, 128 * (m % 8):128 * (m % 8) + 128], pv,
                AF.Copy))
        c.sc(ev)
        prev_ev_act = c.ACT.n
    wkv_ready = c.ACT.n

    vev_marks = {}
    kev_marks = {}
    knr_marks = {}
    p2_dve_ends = []
    p2_st = []
    joins2 = c.marks()
    ld_base = c.LD.n
    for t in range(NT):
        buf = t % 2
        tsub = t % 4

        def p2_load(e, t=t, buf=buf, joins2=joins2, ldn=c.LD.n):
            e.wait_ge(c.LD.h, ldn)
            if t >= 2:
                e.wait_ge(c.DVE.h, p2_dve_ends[t - 2])
                e.wait_ge(c.ACT.h, p2_act_ends_b[t - 2])
            c.LD.inc(e.dma_start(out=x1_t[:, 1024 * buf:1024 * buf + 1024],
                                 in_=x1[128 * t:128 * t + 128, :]), 16)
        c.sy(p2_load)
        ld_now = c.LD.n

        xn_ready = emit_layernorm(c, x1_t[:, 1024 * buf:1024 * buf + 1024],
                                  xn_bfP, lnbP, joins2, extra_ld=ld_now)

        pe_w = [(c.DVE, xn_ready)]
        if t == 0:
            pe_w.append((c.ACT, wkv_ready))
        if t >= 1:
            pe_w.append((c.GDS, p2_st[t - 1]))
        xv = xnfm4[:].rearrange("p (cd k) -> p cd k", k=512)

        def dstfn(tsub=tsub, xv=xv):
            pv = pt_tr[:].rearrange("p (ch k) -> p ch k", k=128)
            return (xv[:, :, 128 * tsub:128 * tsub + 128], pv)
        xnfm_done = emit_transpose8(c, xn_bfP, pt_tr, ident, dstfn,
                                    pe_wait=pe_w)

        def stx(g, t=t, tsub=tsub, xnfm_done=xnfm_done, xv=xv, gsn=c.GDS.n):
            g.wait_ge(c.GDS.h, gsn)
            g.wait_ge(c.ACT.h, xnfm_done)
            c.GDS.inc(g.dma_start(out=xnfm_s[128 * t:128 * t + 128, :],
                                  in_=xv[:, :, 128 * tsub:128 * tsub + 128]),
                      16)
        c.gp(stx)
        p2_st.append(c.GDS.n)

        for g in range(2):
            bank = pk0 if g == 0 else pk1

            def vm(tn, g=g, bank=bank, tsub=tsub, xnfm_done=xnfm_done, t=t):
                tn.wait_ge(c.ACT.h, xnfm_done)
                pm = vev_marks.get((t - 1, g))
                if pm is not None:
                    tn.wait_ge(c.DVE.h, pm)
                km_prev = kev_marks.get((t // 4 - 1, 6 + g))
                if km_prev is not None:
                    tn.wait_ge(c.ACT.h, km_prev)
                kn_prev = knr_marks.get(t // 4 - 1)
                if kn_prev is not None and g == 0:
                    tn.wait_ge(c.DVE.h, kn_prev)
                xv2 = xnfm4[:].rearrange("p (cd k) -> p cd k", k=512)
                for cd in range(8):
                    i = tn.matmul(
                        bank[:], xv2[:, cd, 128 * tsub:128 * tsub + 128],
                        wkvT[:, 8192 + 1024 * cd + 512 * g:
                             8192 + 1024 * cd + 512 * g + 512],
                        start=(cd == 0), stop=(cd == 7))
                c.PE.inc(i)
            c.te(vm)
            vm_done = c.PE.n

            def ev2(v, g=g, bank=bank, t=t, vm_done=vm_done):
                v.wait_ge(c.PE.h, vm_done)
                pv3 = bank[:].rearrange("p (h k) -> p h k", k=64)
                hh = 0 if g == 0 else 8
                c.DVE.inc(v.tensor_copy(v4[:, t, hh:hh + 8, 0:64], pv3))
            c.ve(ev2)
            vev_marks[(t, g)] = c.DVE.n

        if tsub == 3:
            g4 = t // 4
            for co in range(8):
                bank = pk0 if co % 2 == 0 else pk1

                def km(tn, co=co, bank=bank, g4=g4, t=t):
                    if co == 0:
                        tn.wait_ge(c.DVE.h, vev_marks[(t, 1)])
                    pm = kev_marks.get((g4, co - 2))
                    if pm is not None:
                        tn.wait_ge(c.ACT.h, pm)
                    for cd in range(8):
                        i = tn.matmul(
                            bank[:],
                            wkvT[:, 1024 * cd + 128 * co:
                                 1024 * cd + 128 * co + 128],
                            xnfm4[:, 512 * cd:512 * cd + 512],
                            start=(cd == 0), stop=(cd == 7))
                    c.PE.inc(i)
                c.te(km)
                km_done = c.PE.n

                def kev(sx, co=co, bank=bank, g4=g4, km_done=km_done):
                    sx.wait_ge(c.PE.h, km_done)
                    c.ACT.inc(sx.activation(
                        kfm[:, S * co + 512 * g4:S * co + 512 * g4 + 512],
                        bank[:], AF.Copy))
                c.sc(kev)
                kev_marks[(g4, co)] = c.ACT.n

            prev_ind = None
            for co in range(8):
                def ksq(sx, co=co, g4=g4, prev_ind=prev_ind):
                    sx.wait_ge(c.ACT.h, kev_marks[(g4, co)])
                    if prev_ind is not None:
                        sx.wait_ge(c.PE.h, prev_ind)
                    c.ACT.inc(sx.activation(
                        sqk[:, 512 * (co % 2):512 * (co % 2) + 512],
                        kfm[:, S * co + 512 * g4:S * co + 512 * g4 + 512],
                        AF.Square))
                c.sc(ksq)
                ksq_done = c.ACT.n

                def knm(tn, co=co, ksq_done=ksq_done, g4=g4):
                    tn.wait_ge(c.ACT.h, ksq_done)
                    if co == 0:
                        pk = knr_marks.get(g4 - 1)
                        if pk is not None:
                            tn.wait_ge(c.DVE.h, pk)
                    c.PE.inc(tn.matmul(
                        pn[:], ind16[:, 16 * co:16 * co + 16],
                        sqk[:, 512 * (co % 2):512 * (co % 2) + 512],
                        start=(co == 0), stop=(co == 7)))
                c.te(knm)
                prev_ind = c.PE.n
            kn_pe = c.PE.n

            def knr(v, g4=g4, kn_pe=kn_pe):
                v.wait_ge(c.PE.h, kn_pe)
                c.DVE.inc(v.tensor_reduce(kn16[:], pn[:], AX.X, ALU.max))
                v.drain()
                if g4 == 0:
                    c.DVE.inc(v.tensor_copy(kmax2[:], kn16[:]))
                else:
                    c.DVE.inc(v.tensor_tensor(kmax2[:], kmax2[:], kn16[:],
                                              ALU.max))
            c.ve(knr)
            knr_marks[g4] = c.DVE.n

        p2_dve_ends.append(c.DVE.n)
        try:
            p2_act_ends_b.append(c.ACT.n)
        except NameError:
            p2_act_ends_b = [c.ACT.n]
        joins2 = c.marks()
        if tsub == 3:
            joins2 = c.marks()

    p2_end = c.marks()
    p2_xn_st = list(p2_st)
    es2.close()

    es3 = ExitStack()
    sb3 = lambda n, sh, dt: es3.enter_context(nc.sbuf_tensor(n, sh, dt))
    ps3 = lambda n, sh, dt: es3.enter_context(nc.psum_tensor(n, sh, dt))

    wqT = sb3("wqT", [128, 8 * 1024], BF16)
    wtmp3 = sb3("wtmp3", [128, 1024], BF16)
    xnfmQ = sb3("xnfmQ", [128, 4096], BF16)
    qfm = sb3("qfm", [128, 4096], BF16)
    sqq = sb3("sqq", [128, 1024], BF16)
    qn16 = sb3("qn16", [16, 1], F32)
    m16o = sb3("m16o", [16, 1], F32)
    nmP = sb3("nmP", [16, 1], BF16)
    nmrowP = sb3("nmrowP", [1, 16], BF16)
    negm_bcP = sb3("negm_bcP", [128, 16], F32)
    pTA = sb3("pTA", [128, 3072], BF16)
    pTB = sb3("pTB", [128, 3072], BF16)
    pvs = sb3("pvs", [65, 512], BF16)
    attn_acc = sb3("attn_acc", [128, 4096], BF16)
    x12 = sb3("x12", [128, 3072], F32)
    linvP8 = sb3("linvP8", [128, 8], F32)

    psA = ps3("psA", [128, 1536], F32)
    psB = ps3("psB", [128, 1536], F32)
    pvA = ps3("pvA", [128, 512], F32)
    pvB = ps3("pvB", [128, 512], F32)
    pQ = psA[:, 0:512]
    pQn = psA[:, 512:1024]
    pQm = psA[:, 1024:1536]
    ptr3 = psB[:, 0:256].bitcast(BF16)
    ptrA = pvA[:, 0:256].bitcast(BF16)
    ptrB = pvB[:, 0:256].bitcast(BF16)

    wq3 = wqT[:].rearrange("p (cd k) -> p cd k", k=1024)
    prev_tr_pe = c.PE.n
    prev_ev_act = c.ACT.n
    joins3 = p2_end
    for m in range(8):
        def ld(g, m=m, ptp=prev_tr_pe, joins3=joins3):
            wait_marks(c, g, joins3 if m == 0 else None)
            if m >= 1:
                g.wait_ge(c.PE.h, ptp)
            c.GD.inc(g.dma_start(out=wtmp3[:],
                                 in_=wqkv1[128 * m:128 * m + 128, :]), 16)
        c.gp(ld)
        ld_done = c.GD.n

        for bb in range(2):
            def tr(tn, ld_done=ld_done, prev_ev_act=prev_ev_act, m=m,
                   bb=bb, joins3=joins3, prev_ev2=c.ACT.n):
                if m == 0 and bb == 0:
                    wait_marks(c, tn, joins3)
                tn.wait_ge(c.GD.h, ld_done)
                tn.wait_ge(c.ACT.h, prev_ev2)
                for ch in range(4):
                    i = tn.transpose(
                        ptr3[:, 128 * ch:128 * ch + 128],
                        wtmp3[:, 128 * (4 * bb + ch):
                              128 * (4 * bb + ch) + 128], ident[:])
                c.PE.inc(i)
            c.te(tr)

            def ev(sx, m=m, bb=bb, tr_done=c.PE.n):
                sx.wait_ge(c.PE.h, tr_done)
                pv = ptr3[:].rearrange("p (ch k) -> p ch k", k=128)
                c.ACT.inc(sx.activation(
                    wq3[:, 4 * bb:4 * bb + 4, 128 * m:128 * m + 128], pv,
                    AF.Copy))
            c.sc(ev)
        prev_tr_pe = c.PE.n
        prev_ev_act = c.ACT.n
    wq_ready = c.ACT.n

    joins3 = c.marks()
    x2p_st = []
    prev_qb_exp = None
    prev_qm7 = None
    ldq_marks = {}

    def emit_ldq(qb):
        def ldq(e, qb=qb, ldn=c.LD.n, pq7=prev_qm7):
            e.wait_ge(c.LD.h, ldn)
            if qb == 0:
                e.wait_ge(c.GDS.h, p2_xn_st[-1])
                wait_marks(c, e, p2_end)
            else:
                e.wait_ge(c.PE.h, pq7)
            xq = xnfmQ[:].rearrange("p (cd k) -> p cd k", k=512)
            for tsub in range(4):
                t0 = 4 * qb + tsub
                c.LD.inc(e.dma_start(
                    out=xq[:, :, 128 * tsub:128 * tsub + 128],
                    in_=xnfm_s[128 * t0:128 * t0 + 128, :]), 16)
        c.sy(ldq)
        ldq_marks[qb] = c.LD.n

    emit_ldq(0)
    for qb in range(NQB):
        ldq_done = ldq_marks[qb]

        qev_marks = []
        for co in range(8):
            qs = pQ if co % 2 == 0 else pQm

            def qm(tn, co=co, qs=qs, ldq_done=ldq_done, joins3=joins3,
                   qb=qb,
                   qprev=(qev_marks[co - 2] if co >= 2 else None),
                   pqe=prev_qb_exp):
                if co == 0:
                    wait_marks(c, tn, joins3)
                    tn.wait_ge(c.LD.h, ldq_done)
                    if pqe:
                        tn.wait_ge(c.ACT.h, pqe)
                    if qb == 0:
                        tn.wait_ge(c.ACT.h, wq_ready)
                if qprev is not None:
                    tn.wait_ge(c.ACT.h, qprev)
                for cd in range(8):
                    i = tn.matmul(
                        qs, wqT[:, 1024 * cd + 128 * co:
                                   1024 * cd + 128 * co + 128],
                        xnfmQ[:, 512 * cd:512 * cd + 512],
                        start=(cd == 0), stop=(cd == 7))
                c.PE.inc(i)
            c.te(qm)
            qm_done = c.PE.n

            def qev(sx, co=co, qs=qs, qm_done=qm_done):
                sx.wait_ge(c.PE.h, qm_done)
                c.ACT.inc(sx.activation(qfm[:, 512 * co:512 * co + 512],
                                        qs, AF.Copy, scale=0.125))
            c.sc(qev)
            qev_marks.append(c.ACT.n)
        prev_qm7 = qm_done

        qnm_marks = []
        pend_n = None

        def mk_qnm(co, qsq_done):
            def qnm(tn, co=co, qsq_done=qsq_done,
                    qv7=(qev_marks[7] if co == 0 else None)):
                tn.wait_ge(c.PL.h, qsq_done)
                if qv7 is not None:
                    tn.wait_ge(c.ACT.h, qv7)
                c.PE.inc(tn.matmul(
                    pQn[0:16, :], ind16[:, 16 * co:16 * co + 16],
                    sqq[:, 512 * (co % 2):512 * (co % 2) + 512],
                    start=(co == 0), stop=(co == 7)))
            c.te(qnm)
            qnm_marks.append(c.PE.n)

        for co in range(8):
            def qsq(g, co=co, qev=qev_marks[co],
                    pi=(qnm_marks[co - 2] if co >= 2 else None)):
                g.wait_ge(c.ACT.h, qev)
                if pi is not None:
                    g.wait_ge(c.PE.h, pi)
                c.PL.inc(g.tensor_tensor(
                    sqq[:, 512 * (co % 2):512 * (co % 2) + 512],
                    qfm[:, 512 * co:512 * co + 512],
                    qfm[:, 512 * co:512 * co + 512], ALU.mult))
            c.gp(qsq)
            if pend_n is not None:
                mk_qnm(*pend_n)
            pend_n = (co, c.PL.n)
        mk_qnm(*pend_n)
        qn_pe = c.PE.n

        def qnr(v, qn_pe=qn_pe):
            v.wait_ge(c.PE.h, qn_pe)
            c.DVE.inc(v.tensor_reduce(qn16[:], pQn[0:16, :], AX.X, ALU.max))
            v.drain()
            c.DVE.inc(v.scalar_tensor_tensor(qn16[:], qn16[:], 64.0,
                                             kmax2[:], ALU.mult, ALU.add))
        c.ve(qnr)
        nbc_done = emit_negm_bcast(c, qn16, m16o, nmP, nmrowP, ptr3, pQm,
                                   negm_bcP, ident, ones_row, 1.0 / 16.0,
                                   c.DVE.n,
                                   bcm_waits=([(c.ACT, prev_qb_exp)]
                                              if prev_qb_exp else ()),
                                   af=AF.Copy)

        prev_ts = joins3["DVE"]
        prev_hp_exp = None
        batches = [(3 * b, min(3 * b + 3, NT)) for b in range((NT + 2) // 3)]
        for hp in range(8):
            exp_marks = []
            pv_marks = []
            pend = None

            def mk_pvm(hp, b2, k0_, k1_, em, bi, prev_ts):
                def pvm_(tn, hp=hp, b2=b2, k0_=k0_, k1_=k1_, em=em,
                         prev_ts=prev_ts, bi=bi):
                    tn.wait_ge(c.ACT.h, em)
                    if bi == 0:
                        tn.wait_ge(c.DVE.h, prev_ts)
                    for kk in range(k0_, k1_):
                        off = 1536 * b2 + 512 * (kk - k0_)
                        v0 = HV * kk
                        i = tn.matmul(
                            pvA[0:65, :],
                            vtm[:, v0 + VST * 2 * hp:
                                v0 + VST * 2 * hp + VST],
                            pTA[:, off:off + 512],
                            start=(kk == 0), stop=(kk == NT - 1))
                        i = tn.matmul(
                            pvB[0:65, :],
                            vtm[:, v0 + VST * (2 * hp + 1):
                                v0 + VST * (2 * hp + 1) + VST],
                            pTB[:, off:off + 512],
                            start=(kk == 0), stop=(kk == NT - 1))
                    c.PE.inc(i)
                c.te(pvm_)
                pv_marks.append(c.PE.n)

            for bi, (k0_, k1_) in enumerate(batches):
                for kc in range(k0_, k1_):
                    sl = kc % 3

                    def smm(tn, hp=hp, kc=kc, sl=sl, nbc_done=nbc_done,
                            prev_exp=prev_hp_exp,
                            emv=(exp_marks[kc // 3 - 1]
                                 if (kc >= 3 and kc % 3 == 0) else None)):
                        if kc == 0 and hp == 0:
                            tn.wait_ge(c.ACT.h, nbc_done)
                        if kc == 0 and prev_exp is not None:
                            tn.wait_ge(c.ACT.h, prev_exp)
                        if emv is not None:
                            tn.wait_ge(c.ACT.h, emv)
                        k0 = S * hp + 128 * kc
                        i = tn.matmul(psA[:, 512 * sl:512 * sl + 512],
                                      kfm[0:64, k0:k0 + 128],
                                      qfm[0:64, 512 * hp:512 * hp + 512],
                                      start=True, stop=True)
                        i = tn.matmul(psB[:, 512 * sl:512 * sl + 512],
                                      kfm[64:128, k0:k0 + 128],
                                      qfm[64:128, 512 * hp:512 * hp + 512],
                                      start=True, stop=True,
                                      tile_position=(64, 0))
                        c.PE.inc(i)
                    c.te(smm)
                    smm_done = c.PE.n

                b2 = bi % 2
                nb = k1_ - k0_

                def ex(sx, hp=hp, b2=b2, smm_done=smm_done, bi=bi, nb=nb,
                       pvm=(pv_marks[bi - 2] if bi >= 2 else None)):
                    sx.wait_ge(c.PE.h, smm_done)
                    if pvm is not None:
                        sx.wait_ge(c.PE.h, pvm)
                    c.ACT.inc(sx.activation(
                        pTA[:, 1536 * b2:1536 * b2 + 512 * nb],
                        psA[:, 0:512 * nb], AF.Exp,
                        bias=negm_bcP[:, 2 * hp:2 * hp + 1]))
                    c.ACT.inc(sx.activation(
                        pTB[:, 1536 * b2:1536 * b2 + 512 * nb],
                        psB[:, 0:512 * nb], AF.Exp,
                        bias=negm_bcP[:, 2 * hp + 1:2 * hp + 2]))
                c.sc(ex)
                exp_marks.append(c.ACT.n)

                if pend is not None:
                    mk_pvm(*pend)
                pend = (hp, b2, k0_, k1_, c.ACT.n, bi, prev_ts)
            mk_pvm(*pend)

            for j2, (pvx, ptrX, hd) in enumerate(
                    ((pvA, ptrA, 2 * hp), (pvB, ptrB, 2 * hp + 1))):
                def pev(sx, pvx=pvx, pvm=pv_marks[-1], j2=j2,
                        prev_dve=c.DVE.n):
                    sx.wait_ge(c.PE.h, pvm)
                    if j2 == 1:
                        sx.wait_ge(c.DVE.h, prev_dve)
                    c.ACT.inc(sx.activation(pvs[:], pvx[0:65, :], AF.Copy))
                c.sc(pev)
                pev_done = c.ACT.n

                def ptr(tn, ptrX=ptrX, pev_done=pev_done, prev_dve=c.DVE.n):
                    tn.wait_ge(c.ACT.h, pev_done)
                    tn.wait_ge(c.DVE.h, prev_dve)
                    for j in range(4):
                        i = tn.transpose(ptrX[:, 128 * j:128 * j + 65],
                                         pvs[0:65, 128 * j:128 * j + 128],
                                         ident[0:65, 0:65])
                    c.PE.inc(i)
                c.te(ptr)
                ptr_done = c.PE.n

                def nr(v, hd=hd, ptrX=ptrX, j2=j2, ptr_done=ptr_done):
                    v.wait_ge(c.PE.h, ptr_done)
                    o = 4 * j2
                    for j in range(4):
                        i = v.reciprocal(
                            linvP8[:, o + j:o + j + 1],
                            ptrX[:, 128 * j + 64:128 * j + 65])
                    c.DVE.inc(i)
                    v.drain()
                    for j in range(4):
                        i = v.tensor_scalar(
                            attn_acc[:, 1024 * j + 64 * hd:
                                     1024 * j + 64 * hd + 64],
                            ptrX[:, 128 * j:128 * j + 64],
                            linvP8[:, o + j:o + j + 1], None, ALU.mult)
                    c.DVE.inc(i)
                c.ve(nr)
            prev_ts = c.DVE.n
            prev_hp_exp = exp_marks[-1]
        attn_done = c.DVE.n
        prev_qb_exp = prev_hp_exp

        if qb + 1 < NQB:
            emit_ldq(qb + 1)

        for j in range(4):
            t0 = 4 * qb + j
            ji = 1024 * (j % 3)
            jo = 1024 * ((j + 2) % 3)

            def ldx(e, t0=t0, ji=ji, attn_done=attn_done, j=j,
                    st_prev=(x2p_st[-1] if len(x2p_st) >= 1 else None),
                    ldn=c.LD.n):
                e.wait_ge(c.LD.h, ldn)
                if j == 0:
                    e.wait_ge(c.DVE.h, attn_done)
                    if st_prev is not None:
                        e.wait_ge(c.GDS.h, st_prev)
                if j >= 2 and st_prev is not None:
                    e.wait_ge(c.GDS.h, st_prev)
                c.LD.inc(e.dma_start(
                    out=x12[:, ji:ji + 1024],
                    in_=x1[128 * t0:128 * t0 + 128, :]), 16)
            c.sy(ldx)
            ldx_done = c.LD.n

            def addx(v, j=j, ji=ji, jo=jo, ldx_done=ldx_done,
                     attn_done=attn_done):
                v.wait_ge(c.LD.h, ldx_done)
                v.wait_ge(c.DVE.h, attn_done)
                c.DVE.inc(v.tensor_add(
                    x12[:, jo:jo + 1024],
                    x12[:, ji:ji + 1024],
                    attn_acc[:, 1024 * j:1024 * j + 1024]))
            c.ve(addx)
            addx_done = c.DVE.n

            def stx2(g, t0=t0, jo=jo, addx_done=addx_done, gsn=c.GDS.n):
                g.wait_ge(c.GDS.h, gsn)
                g.wait_ge(c.DVE.h, addx_done)
                c.GDS.inc(g.dma_start(
                    out=x2p[128 * t0:128 * t0 + 128, :],
                    in_=x12[:, jo:jo + 1024]), 16)
            c.gp(stx2)
            x2p_st.append(c.GDS.n)
        if DEBUG_L1 and qb == 0:
            def dbg3(g, am=attn_done):
                g.wait_ge(c.DVE.h, am)
                c.GD.inc(g.dma_start(out=io["d3_qfm"], in_=qfm[:]), 16)
                c.GD.inc(g.dma_start(out=io["d3_negm"], in_=negm_bcP[:]), 16)
                c.GD.inc(g.dma_start(out=io["d3_acc"], in_=attn_acc[:]), 16)
                c.GD.inc(g.dma_start(out=io["d3_kmax"], in_=kmax2[:]), 16)
                c.GD.inc(g.dma_start(out=io["d3_kfm"], in_=kfm[:, 0:1024]), 16)
                c.GD.inc(g.dma_start(out=io["d3_vt"], in_=vtm[:, 0:1040]), 16)
            c.gp(dbg3)
        joins3 = c.marks()
        if DEBUG_L1 and qb == 0:
            joins3["GD"] = c.GD.n

    p3_end = c.marks()
    es3.close()
    es23.close()

    es4 = ExitStack()
    sb4 = lambda n, sh, dt: es4.enter_context(nc.sbuf_tensor(n, sh, dt))
    ps4 = lambda n, sh, dt: es4.enter_context(nc.psum_tensor(n, sh, dt))

    wgT = sb4("wgT", [128, 24576], BF16)
    wuT = sb4("wuT", [128, 24576], BF16)
    wdT = sb4("wdT", [128, 24576], BF16)
    hbuf = sb4("hbuf", [128, 3072], BF16)
    ubuf = sb4("ubuf", [128, 3072], BF16)
    hfm = sb4("hfm", [128, 3072], BF16)
    xnf4 = sb4("xnf4", [128, 2048], BF16)
    x2pt = sb4("x2pt", [128, 2048], F32)
    outt = sb4("outt", [128, 2048], F32)

    pg0 = ps4("pg0", [128, 512], F32)
    pg1 = ps4("pg1", [128, 512], F32)
    ptg = ps4("ptg", [128, 3072], BF16)
    pt_tr4 = ps4("pt_tr4", [128, 1024], BF16)

    es4w = ExitStack()
    wtmp_i = es4w.enter_context(nc.sbuf_tensor("wtmp_i", [128, 3072], I32))
    wtmp_b = es4w.enter_context(nc.sbuf_tensor("wtmp_b", [128, 3072], BF16))

    joins4 = p3_end
    prev_tr_pe = c.PE.n
    prev_cvt = c.DVE.n
    for wi, (wsrc, wdst, nchunk, blkw) in enumerate(
            ((wg, wgT, 24, 1024), (wu, wuT, 24, 1024), (wdn, wdT, 8, 3072))):
        wd3 = wdst[:].rearrange("p (cd k) -> p cd k", k=(3072 if blkw == 1024
                                                         else 1024))
        for m in range(nchunk):
            def ld(e, wsrc=wsrc, m=m, blkw=blkw, prev_cvt=prev_cvt,
                   joins4=joins4, first=(wi == 0 and m == 0),
                   stw=(x2p_st[-1] if x2p_st else 0)):
                if first:
                    wait_marks(c, e, joins4)
                    e.wait_ge(c.GDS.h, stw)
                e.wait_ge(c.DVE.h, prev_cvt)
                c.LDW.inc(e.dma_start(
                    out=wtmp_i[:, 0:blkw],
                    in_=wsrc[128 * m:128 * m + 128, :]), 16)
            c.sy(ld)
            ld_done = c.LDW.n

            def cvt(v, ld_done=ld_done, blkw=blkw, prev_tr=prev_tr_pe):
                v.wait_ge(c.LDW.h, ld_done)
                v.wait_ge(c.PE.h, prev_tr)
                c.DVE.inc(v.tensor_copy(wtmp_b[:, 0:blkw],
                                        wtmp_i[:, 0:blkw]))
            c.ve(cvt)
            prev_cvt = c.DVE.n

            nblk = blkw // 128
            for bb in range(0, nblk, 8):
                def tr(tn, bb=bb, nblk=nblk, cvt_done=prev_cvt,
                       prev_ev=c.ACT.n, first=(wi == 0 and m == 0 and bb == 0),
                       joins4=joins4):
                    if first:
                        wait_marks(c, tn, joins4)
                    tn.wait_ge(c.DVE.h, cvt_done)
                    tn.wait_ge(c.ACT.h, prev_ev)
                    for ch in range(bb, min(bb + 8, nblk)):
                        i = tn.transpose(
                            pt_tr4[:, 128 * (ch - bb):128 * (ch - bb) + 128],
                            wtmp_b[:, 128 * ch:128 * ch + 128], ident[:])
                    c.PE.inc(i)
                c.te(tr)
                tr_done = c.PE.n

                def ev(sx, m=m, bb=bb, nblk=nblk, wd3=wd3, tr_done=tr_done):
                    sx.wait_ge(c.PE.h, tr_done)
                    nn = min(8, nblk - bb)
                    pv = pt_tr4[:, 0:128 * nn].rearrange(
                        "p (ch k) -> p ch k", k=128)
                    c.ACT.inc(sx.activation(
                        wd3[:, bb:bb + nn, 128 * m:128 * m + 128], pv,
                        AF.Copy))
                c.sc(ev)
            prev_tr_pe = c.PE.n
    wff_ready = c.ACT.n
    es4w.close()

    joins4 = c.marks()
    out_st = []
    p4_dve_ends = []
    prev_hev = None
    for t in range(NT):
        buf = t % 2

        def ldt(e, t=t, buf=buf, joins4=joins4, ldn=c.LD.n):
            e.wait_ge(c.LD.h, ldn)
            if t == 0:
                wait_marks(c, e, joins4)
            if t >= 2:
                e.wait_ge(c.DVE.h, p4_dve_ends[t - 2])
                e.wait_ge(c.GDS.h, out_st[t - 1])
            c.LD.inc(e.dma_start(
                out=xnf4[:, 1024 * buf:1024 * buf + 1024],
                in_=xnfm_s[128 * t:128 * t + 128, :]), 16)
            c.LD.inc(e.dma_start(
                out=x2pt[:, 1024 * buf:1024 * buf + 1024],
                in_=x2p[128 * t:128 * t + 128, :]), 16)
        c.sy(ldt)
        ldt_done = c.LD.n

        gu_ev = []
        for half, (wmat, dst) in enumerate(((wgT, hbuf), (wuT, ubuf))):
            for g in range(6):
                bank = pg0 if (6 * half + g) % 2 == 0 else pg1

                def gm(tn, g=g, bank=bank, wmat=wmat, buf=buf, t=t,
                       ldt_done=ldt_done, half=half, joins4=joins4,
                       pdv=(p4_dve_ends[t - 1] if t >= 1 else None),
                       prev_ev=(gu_ev[6 * half + g - 2]
                                if 6 * half + g >= 2 else None)):
                    if half == 0 and g == 0:
                        tn.wait_ge(c.LD.h, ldt_done)
                        if t == 0:
                            wait_marks(c, tn, joins4)
                            tn.wait_ge(c.ACT.h, wff_ready)
                        else:
                            tn.wait_ge(c.DVE.h, pdv)
                    if prev_ev is not None:
                        tn.wait_ge(c.ACT.h, prev_ev)
                    for cd in range(8):
                        i = tn.matmul(
                            bank[:],
                            xnf4[:, 1024 * buf + 128 * cd:
                                 1024 * buf + 128 * cd + 128],
                            wmat[:, 3072 * cd + 512 * g:
                                 3072 * cd + 512 * g + 512],
                            start=(cd == 0), stop=(cd == 7))
                    c.PE.inc(i)
                c.te(gm)
                gm_done = c.PE.n

                def gev(sx, g=g, bank=bank, dst=dst, half=half,
                        gm_done=gm_done, prev_dve=c.DVE.n):
                    sx.wait_ge(c.PE.h, gm_done)
                    sx.wait_ge(c.DVE.h, prev_dve)
                    c.ACT.inc(sx.activation(
                        dst[:, 512 * g:512 * g + 512], bank[:],
                        AF.Silu if half == 0 else AF.Copy,
                        scale=(1.0 / 127.0 if half == 0 else 1.0)))
                c.sc(gev)
                gu_ev.append(c.ACT.n)
        gu_done = c.ACT.n

        hm3 = []
        htr3 = []
        hev3 = []
        for i3 in range(3):
            def hm(v, i3=i3, gw=gu_ev[7 + 2 * i3]):
                v.wait_ge(c.ACT.h, gw)
                c.DVE.inc(v.tensor_tensor(
                    hbuf[:, 1024 * i3:1024 * i3 + 1024],
                    hbuf[:, 1024 * i3:1024 * i3 + 1024],
                    ubuf[:, 1024 * i3:1024 * i3 + 1024], ALU.mult))
            c.ve(hm)
            hm3.append(c.DVE.n)

            def htr(tn, i3=i3, hmk=hm3[i3], prev_ev=prev_hev):
                tn.wait_ge(c.DVE.h, hmk)
                if i3 == 0 and prev_ev is not None:
                    tn.wait_ge(c.ACT.h, prev_ev)
                for ch in range(8 * i3, 8 * i3 + 8):
                    i = tn.transpose(ptg[:, 128 * ch:128 * ch + 128],
                                     hbuf[:, 128 * ch:128 * ch + 128],
                                     ident[:])
                c.PE.inc(i)
            c.te(htr)
            htr3.append(c.PE.n)

            def hev(sx, i3=i3, trk=htr3[i3]):
                sx.wait_ge(c.PE.h, trk)
                c.ACT.inc(sx.activation(
                    hfm[:, 1024 * i3:1024 * i3 + 1024],
                    ptg[:, 1024 * i3:1024 * i3 + 1024], AF.Copy))
            c.sc(hev)
            hev3.append(c.ACT.n)
        hm_done = hm3[-1]
        hev_done = hev3[-1]
        prev_hev = hev_done

        for g in range(2):
            bank = pg0 if g == 0 else pg1

            def dm(tn, g=g, bank=bank, hev3=tuple(hev3),
                   prev_dve=c.DVE.n):
                tn.wait_ge(c.DVE.h, prev_dve)
                for hc in range(24):
                    if g == 0 and hc % 8 == 0:
                        tn.wait_ge(c.ACT.h, hev3[hc // 8])
                    i = tn.matmul(
                        bank[:], hfm[:, 128 * hc:128 * hc + 128],
                        wdT[:, 1024 * hc + 512 * g:1024 * hc + 512 * g + 512],
                        start=(hc == 0), stop=(hc == 23))
                c.PE.inc(i)
            c.te(dm)
            dm_done = c.PE.n

            def dev(v, g=g, bank=bank, buf=buf, dm_done=dm_done,
                    ldt_done=ldt_done):
                v.wait_ge(c.PE.h, dm_done)
                v.wait_ge(c.LD.h, ldt_done)
                c.DVE.inc(v.scalar_tensor_tensor(
                    outt[:, 1024 * buf + 512 * g:1024 * buf + 512 * g + 512],
                    bank[:], 1.0 / (127.0 * 127.0),
                    x2pt[:, 1024 * buf + 512 * g:1024 * buf + 512 * g + 512],
                    ALU.mult, ALU.add))
            c.ve(dev)
        outt_done = c.DVE.n

        def sto(g, t=t, buf=buf, outt_done=outt_done, gsn=c.GDS.n):
            g.wait_ge(c.GDS.h, gsn)
            g.wait_ge(c.DVE.h, outt_done)
            c.GDS.inc(g.dma_start(
                out=out[128 * t:128 * t + 128, :],
                in_=outt[:, 1024 * buf:1024 * buf + 1024]), 16)
        c.gp(sto)
        out_st.append(c.GDS.n)
        p4_dve_ends.append(c.DVE.n)
    es4.close()
    return {"end": c.marks()}


def _add_l1_io(nc, io, x1_kind, inner_kinds):
    io["wqkv1"] = nc.dram_tensor("wqkv1", [3 * D, D], F32,
                                 kind="ExternalInput").ap()
    io["wg"] = nc.dram_tensor("wg", [HID, D], I32, kind="ExternalInput").ap()
    io["wu"] = nc.dram_tensor("wu", [HID, D], I32, kind="ExternalInput").ap()
    io["wdn"] = nc.dram_tensor("wdn", [D, HID], I32,
                               kind="ExternalInput").ap()
    io["ind16"] = nc.dram_tensor("ind16", [128, 128], BF16,
                                 kind="ExternalInput").ap()
    io["out"] = nc.dram_tensor("out", [S, D], F32,
                               kind="ExternalOutput").ap()
    kk = {"kind": "ExternalOutput"} if DEBUG_L1 else {}
    io["xnfm_s"] = nc.dram_tensor("xnfm_s", [S, D], BF16, **kk).ap()
    io["x2p"] = nc.dram_tensor("x2p", [S, D], F32, **kk).ap()


def build_l1_program():
    nc = bass.Bass("TRN2", target_bir_lowering=False)
    io = {}
    io["x1"] = nc.dram_tensor("x1", [S, D], F32, kind="ExternalInput").ap()
    io["ident"] = nc.dram_tensor("ident", [128, 128], BF16,
                                 kind="ExternalInput").ap()
    _add_l1_io(nc, io, "ExternalInput", "Internal")
    es = ExitStack()
    with es:
        with nc.Block() as block:
            c = Ctx(nc, block, es)
            build_l1(nc, c, io, es, x1_gd=None)
            def fin(e, n=c.GDS.n, n2=c.GD.n):
                e.wait_ge(c.GDS.h, n)
                e.wait_ge(c.GD.h, n2)
            c.sy(fin)
            c.flush()
    return nc


def build_full_program():
    nc = bass.Bass("TRN2", target_bir_lowering=False)
    io = {}
    io["x"] = nc.dram_tensor("x", [S, D], F32, kind="ExternalInput").ap()
    io["w_qkv"] = nc.dram_tensor("w_qkv", [D, 3 * D], F32,
                                 kind="ExternalInput").ap()
    io["w_o"] = nc.dram_tensor("w_o", [D, D], F32, kind="ExternalInput").ap()
    io["fw"] = nc.dram_tensor("fw", [D, D], F32, kind="ExternalInput").ap()
    io["fb"] = nc.dram_tensor("fb", [1, D], F32, kind="ExternalInput").ap()
    io["cosr"] = nc.dram_tensor("cosr", [S, 512], F32,
                                kind="ExternalInput").ap()
    io["sinr"] = nc.dram_tensor("sinr", [S, 512], F32,
                                kind="ExternalInput").ap()
    io["ident"] = nc.dram_tensor("ident", [128, 128], BF16,
                                 kind="ExternalInput").ap()
    io["m_old"] = nc.dram_tensor("m_old", [128, 128], BF16,
                                 kind="ExternalInput").ap()
    io["m_diag"] = nc.dram_tensor("m_diag", [128, 128], BF16,
                                  kind="ExternalInput").ap()
    io["x1"] = nc.dram_tensor("x1", [S, D], F32).ap()
    _add_l1_io(nc, io, None, None)
    es = ExitStack()
    with es:
        with nc.Block() as block:
            c = Ctx(nc, block, es)
            es0 = ExitStack()
            marks = build_l0(nc, c, io, es0)
            es0.close()
            build_l1(nc, c, io, es, x1_gd=marks["x1_store_gd"],
                     joins0=marks["end"])
            def fin(e, n=c.GDS.n, n2=c.GD.n):
                e.wait_ge(c.GDS.h, n)
                e.wait_ge(c.GD.h, n2)
            c.sy(fin)
            c.flush()
    return nc


_CONSTS = None


def kernel(L0_attn_x, L0_attn_w_qkv, L0_attn_w_o, L0_ffn_weight, L0_ffn_bias,
           L1_attn_W_qkv, L1_ffn_w_gate, L1_ffn_w_up, L1_ffn_w_down):
    global _CONSTS
    if _CONSTS is None:
        _CONSTS = host_constants()
    consts = _CONSTS
    x_full = np.ascontiguousarray(np.asarray(L0_attn_x, dtype=np.float32))
    shared = {
        "w_qkv": perm_qkv_cols(np.asarray(L0_attn_w_qkv, np.float32)),
        "w_o": np.ascontiguousarray(np.asarray(L0_attn_w_o, np.float32)),
        "fw": np.ascontiguousarray(np.asarray(L0_ffn_weight, np.float32)),
        "fb": np.ascontiguousarray(
            np.asarray(L0_ffn_bias, np.float32).reshape(1, D)),
        "wqkv1": np.ascontiguousarray(np.asarray(L1_attn_W_qkv, np.float32)),
        "wg": np.ascontiguousarray(np.asarray(L1_ffn_w_gate, np.int32)),
        "wu": np.ascontiguousarray(np.asarray(L1_ffn_w_up, np.int32)),
        "wdn": np.ascontiguousarray(np.asarray(L1_ffn_w_down, np.int32)),
        "cosr": consts["cosr"], "sinr": consts["sinr"],
        "ident": consts["ident"], "m_old": consts["m_old"],
        "m_diag": consts["m_diag"], "ind16": consts["ind16"],
    }
    if LAUNCH_PLAN == ("full",):
        nc = build_full_program()
        in_maps = [{**shared, "x": np.ascontiguousarray(x_full[b])}
                   for b in range(8)]
        res = run_bass_kernel_spmd(nc, in_maps, core_ids=list(range(8)))
        return np.stack([r["out"] for r in res.results], axis=0)
    else:
        nc0 = build_l0_program(x1_kind="ExternalOutput")
        l0_keys = ("x", "w_qkv", "w_o", "fw", "fb", "cosr", "sinr", "ident",
                   "m_old", "m_diag")
        in_maps = [{**{k: shared[k] for k in l0_keys if k in shared},
                    "x": np.ascontiguousarray(x_full[b])} for b in range(8)]
        res0 = run_bass_kernel_spmd(nc0, in_maps, core_ids=list(range(8)))
        x1s = [r["x1"] for r in res0.results]
        nc1 = build_l1_program()
        l1_keys = ("wqkv1", "wg", "wu", "wdn", "ident", "ind16")
        in_maps1 = [{**{k: shared[k] for k in l1_keys}, "x1": x1s[b]}
                    for b in range(8)]
        res1 = run_bass_kernel_spmd(nc1, in_maps1, core_ids=list(range(8)))
        return np.stack([r["out"] for r in res1.results], axis=0)




# version 29 (final)
# speedup vs baseline: 1.2151x; 1.2151x over previous
import numpy as np
import ml_dtypes
from contextlib import ExitStack

import concourse.bass as bass
import concourse.mybir as mybir
from concourse.bass_utils import run_bass_kernel_spmd

F32 = mybir.dt.float32
BF16 = mybir.dt.bfloat16
I32 = mybir.dt.int32
AF = mybir.ActivationFunctionType
ALU = mybir.AluOpType
AX = mybir.AxisListType

S = 4096
D = 1024
NT = S // 128
H = 16
DK = 64
HID = 3072
C_MAGIC = float(3 << 22)
BASS_DEBUG = False
DEBUG_TILE = -1
DEBUG_L1 = False
LAUNCH_PLAN = ("full",)
VST = 65
NQB = 8
HV = H * VST


class Sem:
    def __init__(self, nc, es, name, owner=""):
        self.h = es.enter_context(nc.semaphore(name))
        self.n = 0
        self.owner = owner

    def inc(self, instr, k=1):
        instr.then_inc(self.h, k)
        self.n += k
        return instr


class _MockInstr:
    def then_inc(self, sem, k=1, **kw):
        return self

    def annotate(self, *a, **kw):
        return self


class _MockEngine:
    _MI = _MockInstr()

    def __getattr__(self, name):
        def f(*a, **kw):
            return _MockEngine._MI
        return f


_MOCK = _MockEngine()


class Ctx:

    def __init__(self, nc, block, es):
        self.nc, self.block, self.es = nc, block, es
        self.LD = Sem(nc, es, "sLD", "sync")
        self.LDW = Sem(nc, es, "sLDW", "sync")
        self.GD = Sem(nc, es, "sGD", "gpsimd")
        self.GDS = Sem(nc, es, "sGDS", "gpsimd")
        self.PE = Sem(nc, es, "sPE", "tensor")
        self.ACT = Sem(nc, es, "sACT", "scalar")
        self.DVE = Sem(nc, es, "sDVE", "vector")
        self.PL = Sem(nc, es, "sPL", "gpsimd")
        self.queues = {"sync": [], "tensor": [], "vector": [], "scalar": [],
                       "gpsimd": []}

    MERGE = ("sync", "tensor", "vector", "scalar", "gpsimd")
    CHUNK = {}

    def _submit(self, name, f):
        if name in Ctx.MERGE:
            self.queues[name].append(f)
            f(_MOCK)
        else:
            getattr(self.block, name)(f)

    def sy(self, f):
        self._submit("sync", f)

    def te(self, f):
        self._submit("tensor", f)

    def ve(self, f):
        self._submit("vector", f)

    def sc(self, f):
        self._submit("scalar", f)

    def gp(self, f):
        self._submit("gpsimd", f)

    def flush(self):
        if not Ctx.MERGE:
            return
        for sem in (self.LD, self.LDW, self.GD, self.GDS, self.PE, self.ACT,
                    self.DVE, self.PL):
            if sem.owner in Ctx.MERGE:
                sem.n = 0
        for name in Ctx.MERGE:
            fs = self.queues[name]
            chunk = Ctx.CHUNK.get(name, len(fs)) or len(fs)
            for j in range(0, len(fs), chunk):
                part = fs[j:j + chunk]

                def body(eng, part=part):
                    for f in part:
                        f(eng)
                getattr(self.block, name)(body)

    def marks(self):
        return {"PE": self.PE.n, "ACT": self.ACT.n, "DVE": self.DVE.n,
                "PL": self.PL.n}


def wait_marks(c, eng, marks):
    if marks is None:
        return
    for name, val in marks.items():
        sem = getattr(c, name)
        if val > 0:
            eng.wait_ge(sem.h, val)



def emit_layernorm(c, x_t, xn_bf, b, joins, extra_ld=None,
                   vec_waits=(), sca_waits=()):
    mean, var, msq, rstd, sq = b["mean"], b["var"], b["msq"], b["rstd"], b["sq"]
    nmean = b["nmean"]

    def vec(v):
        wait_marks(c, v, joins)
        for sem, val in vec_waits:
            v.wait_ge(sem.h, val)
        if extra_ld is not None:
            v.wait_ge(c.LD.h, extra_ld)
        c.DVE.inc(v.reduce_sum(mean[:], x_t, AX.X))
        v.drain()
        v.scalar_tensor_tensor(msq[:], mean[:], 1.0 / (D * D), mean[:],
                               ALU.mult, ALU.mult)
        c.DVE.inc(v.tensor_scalar(nmean[:], mean[:], -1.0 / D, None,
                                  ALU.mult))
        v.drain()
    c.ve(vec)

    def sca(g):
        wait_marks(c, g, joins)
        for sem, val in sca_waits:
            g.wait_ge(sem.h, val)
        if extra_ld is not None:
            g.wait_ge(c.LD.h, extra_ld)
        c.PL.inc(g.tensor_tensor(sq[:], x_t, x_t, ALU.mult))
    c.gp(sca)
    sq_done = c.PL.n

    def vec2(v, sq_done=sq_done):
        v.wait_ge(c.PL.h, sq_done)
        c.DVE.inc(v.reduce_sum(var[:], sq[:], AX.X))
        v.drain()
        c.DVE.inc(v.tensor_scalar(var[:], var[:], 1.0 / D, 1e-5, ALU.mult,
                                  ALU.add))
        v.drain()
        c.DVE.inc(v.tensor_sub(var[:], var[:], msq[:]))
    c.ve(vec2)
    var_done = c.DVE.n

    def sca2(s, var_done=var_done):
        s.wait_ge(c.DVE.h, var_done)
        c.ACT.inc(s.activation(rstd[:], var[:], AF.Sqrt))
    c.sc(sca2)
    std_done = c.ACT.n

    def vec3(v, std_done=std_done):
        v.wait_ge(c.ACT.h, std_done)
        c.DVE.inc(v.reciprocal(b["rstd2"][:], rstd[:]))
        v.drain()
        c.DVE.inc(v.tensor_scalar(xn_bf[:], x_t, nmean[:, 0:1],
                                  b["rstd2"][:, 0:1], ALU.add, ALU.mult))
    c.ve(vec3)
    return c.DVE.n


def emit_transpose8(c, src_bf, pt_tr, ident, dst_slice_fn, pe_wait=None,
                    evac_wait=None):
    def ten(t):
        for sem, val in (pe_wait or []):
            t.wait_ge(sem.h, val)
        for ch in range(8):
            i = t.transpose(pt_tr[:, 128 * ch:128 * ch + 128],
                            src_bf[:, 128 * ch:128 * ch + 128], ident[:])
        c.PE.inc(i)
    c.te(ten)
    tr_done = c.PE.n

    def sca(s, tr_done=tr_done):
        s.wait_ge(c.PE.h, tr_done)
        for sem, val in (evac_wait or []):
            s.wait_ge(sem.h, val)
        dst, src = dst_slice_fn()
        c.ACT.inc(s.activation(dst, src, AF.Copy))
    c.sc(sca)
    return c.ACT.n


def emit_quant_fp6(c, src_bf, xd_bf, qb, consts, dve_wait, joins=None):
    am, scale_t, inv = qb["am"], qb["scale"], qb["inv"]
    y, c2, c4, r1, r2, r3, ra = (qb["y"], qb["c2"], qb["c4"], qb["r1"],
                                 qb["r2"], qb["r3"], qb["ra"])
    Ccol, nC8, nC4, nC2 = (consts["Ccol"], consts["nC8"], consts["nC4"],
                           consts["nC2"])
    x3 = src_bf[:].rearrange("p (b k) -> p b k", k=32)
    y3 = y.rearrange("p (b k) -> p b k", k=32) if hasattr(y, "rearrange") else None

    def vec(v):
        wait_marks(c, v, joins)
        v.wait_ge(c.DVE.h, dve_wait)
        c.DVE.inc(v.tensor_reduce(am[:], x3, AX.X, ALU.max,
                                  apply_absolute_value=True))
    c.ve(vec)
    am_done = c.DVE.n

    def sca0(sx, am_done=am_done):
        sx.wait_ge(c.DVE.h, am_done)
        c.ACT.inc(sx.activation(scale_t[:], am[:], AF.Identity,
                                bias=consts["eC"][:, 0:1], scale=1.0 / 7.5))
    c.sc(sca0)
    inv_done = c.ACT.n

    def vecb(v, inv_done=inv_done):
        v.wait_ge(c.ACT.h, inv_done)
        c.DVE.inc(v.reciprocal(inv[:], scale_t[:]))
        v.drain()
        inv3 = inv[:].rearrange("p (b o) -> p b o", o=1).broadcast_to(
            [128, 32, 32])
        yv = y.rearrange("p (b k) -> p b k", k=32)
        c.DVE.inc(v.tensor_tensor(yv, x3, inv3, ALU.mult))
        v.drain()
        v.tensor_scalar(c2, y, 2.0, -2.0, ALU.min, ALU.max)
        c.DVE.inc(v.tensor_scalar(c4, y, 4.0, -4.0, ALU.min, ALU.max))
    c.ve(vecb)
    clamps = c.DVE.n

    def pool(g, clamps=clamps):
        wait_marks(c, g, joins)
        g.wait_ge(c.DVE.h, clamps)
        c.PL.inc(g.tensor_sub(r2, c4, c2))
        c.PL.inc(g.tensor_sub(r3, y, c4))
    c.gp(pool)
    e2_done, e3_done = c.PL.n - 1, c.PL.n

    rb, rc = consts["rb"], consts["rc"]

    def sca(s, clamps=clamps, e2_done=e2_done, e3_done=e3_done):
        s.wait_ge(c.DVE.h, clamps)
        s.activation(ra, c2, AF.Identity, bias=Ccol[:, 0:1], scale=8.0)
        s.wait_ge(c.PL.h, e2_done)
        s.activation(rb, r2, AF.Identity, bias=Ccol[:, 0:1], scale=4.0)
        s.wait_ge(c.PL.h, e3_done)
        c.ACT.inc(s.activation(rc, r3, AF.Identity, bias=Ccol[:, 0:1],
                               scale=2.0))
        s.drain()
        s.activation(r1, ra, AF.Identity, bias=nC8[:, 0:1], scale=0.125)
        s.activation(r2, rb, AF.Identity, bias=nC4[:, 0:1], scale=0.25)
        c.ACT.inc(s.activation(r3, rc, AF.Identity, bias=nC2[:, 0:1],
                               scale=0.5))
    c.sc(sca)
    r_done = c.ACT.n

    def vec2(v, r_done=r_done):
        v.wait_ge(c.ACT.h, r_done)
        c.DVE.inc(v.tensor_add(y, r1, r2))
        v.drain()
        c.DVE.inc(v.tensor_add(y, y, r3))
        v.drain()
        sc3 = scale_t[:].rearrange("p (b o) -> p b o", o=1).broadcast_to(
            [128, 32, 32])
        yv = y.rearrange("p (b k) -> p b k", k=32)
        xdv = xd_bf[:].rearrange("p (b k) -> p b k", k=32)
        c.DVE.inc(v.tensor_tensor(xdv, yv, sc3, ALU.mult))
    c.ve(vec2)
    return c.DVE.n


def emit_negm_bcast(c, m16_src, m16_out, nm_bf, nmrow, pt_small, psmall,
                    negm_bc, ident, ones_row, sqrt_scale, dve_wait,
                    bcm_waits=(), af=None):
    if af is None:
        af = AF.Sqrt
    def sca(s):
        s.wait_ge(c.DVE.h, dve_wait)
        c.ACT.inc(s.activation(m16_out[:], m16_src[:], af,
                               scale=sqrt_scale))
    c.sc(sca)
    m_done = c.ACT.n

    def vec(v, m_done=m_done):
        v.wait_ge(c.ACT.h, m_done)
        c.DVE.inc(v.tensor_scalar(nm_bf[:], m16_out[:], -1.0, None, ALU.mult))
        v.drain()
    c.ve(vec)
    nm_done = c.DVE.n

    def ten(t, nm_done=nm_done):
        t.wait_ge(c.DVE.h, nm_done)
        c.PE.inc(t.transpose(pt_small[0:1, 0:16], nm_bf[:], ident[0:16, 0:16]))
    c.te(ten)
    t1 = c.PE.n

    def vec2(v, t1=t1):
        v.wait_ge(c.PE.h, t1)
        c.DVE.inc(v.tensor_copy(nmrow[:], pt_small[0:1, 0:16]))
        v.drain()
    c.ve(vec2)
    nr_done = c.DVE.n

    def ten2(t, nr_done=nr_done):
        t.wait_ge(c.DVE.h, nr_done)
        for sem, val in bcm_waits:
            t.wait_ge(sem.h, val)
        c.PE.inc(t.matmul(psmall[:, 0:16], ones_row[:], nmrow[:],
                          start=True, stop=True))
    c.te(ten2)
    t2 = c.PE.n

    def sca2(s, t2=t2):
        s.wait_ge(c.PE.h, t2)
        c.ACT.inc(s.activation(negm_bc[:], psmall[:, 0:16], AF.Copy))
    c.sc(sca2)
    return c.ACT.n



def build_l0(nc, c, io, es):
    x_in, w_qkv, w_o, fw, fb = io["x"], io["w_qkv"], io["w_o"], io["fw"], io["fb"]
    cosr, sinr = io["cosr"], io["sinr"]
    identc, m_oldc, m_diagc = io["ident"], io["m_old"], io["m_diag"]
    x1 = io["x1"]

    sb = lambda n, sh, dt: es.enter_context(nc.sbuf_tensor(n, sh, dt))
    ps = lambda n, sh, dt: es.enter_context(nc.psum_tensor(n, sh, dt))

    wqkv = sb("wqkv", [128, 8 * 3072], BF16)
    wo = sb("wo", [128, 8 * 1024], BF16)
    wdq = sb("wdq", [128, 8 * 1024], BF16)
    fbias = sb("fbias", [1, 1024], BF16)
    ident = sb("identS", [128, 128], BF16)
    sc_bf = sb("sc_bf", [128, 1024], F32)
    mbO = sb("mbO", [128, 128], BF16)
    mbD = sb("mbD", [128, 128], BF16)
    identF = sb("identF", [128, 128], F32)
    ones_row = sb("ones_row", [1, 128], BF16)
    Ccol = sb("Ccol", [128, 1], F32)
    nC8 = sb("nC8", [128, 1], F32)
    nC4 = sb("nC4", [128, 1], F32)
    nC2 = sb("nC2", [128, 1], F32)
    eC = sb("eC", [128, 1], F32)
    rbq = sb("rbq", [128, 1024], F32)
    rcq = sb("rcq", [128, 1024], F32)
    consts = {"Ccol": Ccol, "nC8": nC8, "nC4": nC4, "nC2": nC2,
              "eC": eC, "rb": rbq[:], "rc": rcq[:]}

    krr = sb("krr", [128, 6 * 1024], BF16)
    vrr = sb("vrr", [128, 6 * HV], BF16)
    vr4 = vrr[:].rearrange("p (s h v) -> p s h v", s=6, v=VST)

    x_t = sb("x_t", [128, 3 * 1024], F32)
    cos_t = sb("cos_t", [128, 2 * 512], F32)
    sin_t = sb("sin_t", [128, 2 * 512], F32)
    qk_t = sb("qk_t", [128, 2048], F32)
    qr_t = sb("qr_t", [128, 1024], F32)
    kr_t = sb("kr_t", [128, 1024], F32)
    sq = sb("sq", [128, 1024], F32)
    sq2 = sb("sq2", [128, 1024], F32)
    qbuf = sb("qbuf", [128, 2048], F32)
    xn_bf = sb("xn_bf", [128, 1024], BF16)
    xn_fm = sb("xn_fm", [128, 1024], BF16)
    qr_bf = sb("qr_bf", [128, 1024], BF16)
    kr_bf = sb("kr_bf", [128, 1024], BF16)
    q_fm = sb("q_fm", [128, 2048], BF16)
    ao_fm = sb("ao_fm", [128, 2048], BF16)
    xd_bf = sb("xd_bf", [128, 1024], BF16)
    xd_fm = sb("xd_fm", [128, 3072], BF16)
    pT_a = sb("pT_a", [128, 1280], BF16)
    pT_b = sb("pT_b", [128, 1280], BF16)
    mean = sb("mean", [128, 1], F32)
    var = sb("var", [128, 1], F32)
    msq = sb("msq", [128, 1], F32)
    rstd = sb("rstd", [128, 1], F32)
    rstd2 = sb("rstd2", [128, 1], F32)
    am = sb("am", [128, 32], F32)
    scale_t = sb("scale_t", [128, 32], F32)
    inv = sb("inv", [128, 32], F32)
    qn2 = sb("qn2", [128, 16], F32)
    kn2s = sb("kn2s", [128, 16], F32)
    qn2_bf = sb("qn2_bf", [128, 16], BF16)
    kn2_bf = sb("kn2_bf", [128, 16], BF16)
    knhist = sb("knhist", [16, 32], F32)
    w16 = sb("w16", [16, 1], F32)
    w16b = sb("w16b", [16, 1], F32)
    w16c = sb("w16c", [16, 1], F32)
    nm_bf = sb("nm_bf", [16, 1], BF16)
    nmrow = sb("nmrow", [1, 16], BF16)
    negm_bc = sb("negm_bc", [128, 32], F32)
    linv_all = sb("linv_all", [128, 1024], BF16)
    ones_all = sb("ones_all", [128, 64], BF16)
    fwq_bf = sb("fwq_bf", [128, 1024], BF16)

    nmean = sb("nmean", [128, 1], F32)
    lnb = {"mean": mean, "var": var, "msq": msq, "rstd": rstd,
           "rstd2": rstd2, "sq": sq, "nmean": nmean}
    qb = {"am": am, "scale": scale_t, "inv": inv,
          "y": qk_t[:, 0:1024], "c2": qk_t[:, 1024:2048], "c4": qr_t[:],
          "r1": kr_t[:], "r2": sq[:], "r3": qbuf[:, 1024:2048],
          "ra": qbuf[:, 0:1024]}

    ptA_ps = ps("ptA", [128, 512], F32)
    pt_tr = ptA_ps[:].bitcast(BF16)
    ptB_ps = ps("ptB", [128, 512], F32)
    ptB_tr = ptB_ps[:].bitcast(BF16)
    ppv = ptB_ps
    SC = ps("SCb", [128, 1536], F32)
    qv0 = ps("qv0", [128, 512], F32)
    qv1 = ps("qv1", [128, 512], F32)
    pq0 = ps("pq0b", [128, 512], F32)

    def prep_pool(g):
        c.PL.inc(g.memset(ones_row[:], 1.0))
        c.PL.inc(g.memset(Ccol[:], C_MAGIC))
        c.PL.inc(g.memset(nC8[:], -C_MAGIC / 8.0))
        c.PL.inc(g.memset(nC4[:], -C_MAGIC / 4.0))
        c.PL.inc(g.memset(nC2[:], -C_MAGIC / 2.0))
        c.PL.inc(g.memset(eC[:], 1e-30))
        c.PL.inc(g.memset(vr4[:, :, :, 64:65], 1.0))
        c.PL.inc(g.memset(ones_all[:], 1.0))
        for ch in range(8):
            c.GD.inc(g.dma_start(out=wqkv[:, 3072 * ch:3072 * ch + 3072],
                                 in_=w_qkv[128 * ch:128 * ch + 128, :]), 16)
        for ch in range(8):
            c.GD.inc(g.dma_start(out=wo[:, 1024 * ch:1024 * ch + 1024],
                                 in_=w_o[128 * ch:128 * ch + 128, :]), 16)
        c.GD.inc(g.dma_start(out=fbias[:], in_=fb), 16)
    c.gp(prep_pool)
    prep_gd = c.GD.n
    prep_pl = c.PL.n

    def prep_sync(e):
        c.LD.inc(e.dma_start(out=ident[:], in_=identc), 16)
        c.LD.inc(e.dma_start(out=mbO[:], in_=m_oldc), 16)
        c.LD.inc(e.dma_start(out=mbD[:], in_=m_diagc), 16)
        c.LD.inc(e.dma_start(out=x_t[:, 0:1024], in_=x_in[0:128, :]), 16)
        c.LD.inc(e.dma_start(out=cos_t[:, 0:512], in_=cosr[0:128, :]), 16)
        c.LD.inc(e.dma_start(out=sin_t[:, 0:512], in_=sinr[0:128, :]), 16)
        c.LD.inc(e.dma_start(out=x_t[:, 1024:2048], in_=x_in[128:256, :]), 16)
        c.LD.inc(e.dma_start(out=cos_t[:, 512:1024], in_=cosr[128:256, :]), 16)
        c.LD.inc(e.dma_start(out=sin_t[:, 512:1024], in_=sinr[128:256, :]), 16)
    c.sy(prep_sync)
    const_ld = c.LD.n

    def prep_if(sx, cl=const_ld):
        sx.wait_ge(c.LD.h, cl)
        c.ACT.inc(sx.activation(identF[:], ident[:], AF.Copy))
    c.sc(prep_if)

    def prep_mb(v, cl=const_ld):
        v.wait_ge(c.LD.h, cl)
        v.tensor_scalar(mbD[:], mbD[:], 1e30, -1e30, ALU.mult, ALU.add)
        v.drain()
        c.DVE.inc(v.tensor_scalar(mbO[:], mbO[:], 1e30, -1e30,
                                  ALU.mult, ALU.add))
        v.drain()
    c.ve(prep_mb)
    mb_ready = c.DVE.n

    fw_t = x_t[:, 2048:3072]
    for ch in range(8):
        prev_q = c.DVE.n

        def ld(e, ch=ch, prev_tr=c.PE.n, ppl=prep_pl, pcv=c.DVE.n):
            if ch == 0:
                e.wait_ge(c.PL.h, ppl)
            else:
                e.wait_ge(c.DVE.h, pcv)
            c.LDW.inc(e.dma_start(out=fw_t,
                                  in_=fw[128 * ch:128 * ch + 128, :]), 16)
        c.sy(ld)
        ld_done = c.LDW.n

        def cvt(v, ld_done=ld_done, prev_tr=c.PE.n):
            v.wait_ge(c.LDW.h, ld_done)
            v.wait_ge(c.PE.h, prev_tr)
            c.DVE.inc(v.tensor_copy(fwq_bf[:], fw_t))
        c.ve(cvt)
        q_done = emit_quant_fp6(c, fwq_bf, fwq_bf, qb, consts, c.DVE.n)

        def ten(t, q_done=q_done, prev_ev=c.ACT.n, cl=const_ld):
            t.wait_ge(c.DVE.h, q_done)
            t.wait_ge(c.ACT.h, prev_ev)
            t.wait_ge(c.LD.h, cl)
            for d in range(8):
                i = t.transpose(pt_tr[:, 128 * d:128 * d + 128],
                                fwq_bf[:, 128 * d:128 * d + 128], ident[:])
            c.PE.inc(i)
        c.te(ten)
        tr_done = c.PE.n

        def ev(s, ch=ch, tr_done=tr_done):
            s.wait_ge(c.PE.h, tr_done)
            pv = pt_tr[:].rearrange("p (d k) -> p d k", k=128)
            dv = wdq[:].rearrange("p (d o) -> p d o", o=1024)
            c.ACT.inc(s.activation(dv[:, :, 128 * ch:128 * ch + 128], pv,
                                   AF.Copy))
        c.sc(ev)
    wq_done_act = c.ACT.n
    fw_lds = c.LD.n
    fw_prep_dve = c.DVE.n

    joins = c.marks()
    st_counts = {}
    ld_counts = {}
    am = {}
    bm = {}

    def emit_pre(tp):
        def pre(e, tp=tp, ldn=c.LD.n,
                qro=(am[tp]["qrope"], am[tp]["krope"]),
                stw=(st_counts.get(tp - 1)), fpd=fw_prep_dve):
            if tp + 2 < NT:
                e.wait_ge(c.LD.h, ldn)
                if tp == 0:
                    e.wait_ge(c.DVE.h, fpd)
                e.wait_ge(c.DVE.h, qro[0])
                e.wait_ge(c.PL.h, qro[1])
                if stw is not None:
                    e.wait_ge(c.GDS.h, stw)
                nb3 = (tp + 2) % 3
                nb = (tp + 2) % 2
                r0 = 128 * (tp + 2)
                c.LD.inc(e.dma_start(
                    out=x_t[:, 1024 * nb3:1024 * nb3 + 1024],
                    in_=x_in[r0:r0 + 128, :]), 16)
                c.LD.inc(e.dma_start(
                    out=cos_t[:, 512 * nb:512 * nb + 512],
                    in_=cosr[r0:r0 + 128, :]), 16)
                c.LD.inc(e.dma_start(
                    out=sin_t[:, 512 * nb:512 * nb + 512],
                    in_=sinr[r0:r0 + 128, :]), 16)
        c.sy(pre)

    lnm = {}

    def emit_LN(t):
        buf3 = t % 3
        ld_counts[t] = c.LD.n
        pa = am.get(t - 1)
        ln_vw = [(c.PE, pa["xn_tr"])] if pa else ()
        ln_sw = [(c.DVE, pa["qadd"])] if pa else ()
        lnm[t] = emit_layernorm(
            c, x_t[:, 1024 * buf3:1024 * buf3 + 1024],
            xn_bf, lnb, joins if t == 0 else None,
            extra_ld=ld_counts[t],
            vec_waits=ln_vw, sca_waits=ln_sw)

    def emit_A(t):
        buf3 = t % 3
        buf = t % 2
        slot = t % 6
        pa = am.get(t - 1)
        pb = bm.get(t - 2)
        xn_ready = lnm[t]

        pe_w = [(c.DVE, xn_ready)]
        if t == 0:
            pe_w.append((c.ACT, wq_done_act))
        else:
            pe_w.append((c.ACT, pa["xdfm"]))
        xnfm_ready = emit_transpose8(
            c, xn_bf, pt_tr, ident,
            lambda: (xn_fm[:], pt_tr[:]), pe_wait=pe_w)
        xn_tr_pe = c.PE.n

        evac_marks = []
        for g in range(6):
            bank = qv0[:] if g % 2 == 0 else qv1[:]

            def mm(tn, g=g, bank=bank, xnfm_ready=xnfm_ready, t=t,
                   pj=(None if t > 0 else joins["DVE"]),
                   pjp=(pa["em5"] if pa else None),
                   pnb=(pa["nbc"] if pa else None),
                   em2=(evac_marks[g - 2] if g >= 2 else None)):
                if g == 0:
                    tn.wait_ge(c.ACT.h, xnfm_ready)
                    if t == 0:
                        tn.wait_ge(c.GD.h, prep_gd)
                        tn.wait_ge(c.DVE.h, pj)
                    else:
                        tn.wait_ge(pjp[0].h, pjp[1])
                        tn.wait_ge(c.ACT.h, pnb)
                if em2 is not None:
                    tn.wait_ge(em2[0].h, em2[1])
                for ch in range(8):
                    i = tn.matmul(
                        bank, xn_fm[:, 128 * ch:128 * ch + 128],
                        wqkv[:, 3072 * ch + 512 * g:3072 * ch + 512 * g + 512],
                        start=(ch == 0), stop=(ch == 7))
                c.PE.inc(i)
            c.te(mm)
            mm_done = c.PE.n

            if g < 4:
                def ev(sx, g=g, bank=bank, mm_done=mm_done):
                    sx.wait_ge(c.PE.h, mm_done)
                    c.ACT.inc(sx.activation(qk_t[:, 512 * g:512 * g + 512],
                                            bank, AF.Copy))
                c.sc(ev)
                evac_marks.append((c.ACT, c.ACT.n))
            else:
                def ev(v, g=g, bank=bank, mm_done=mm_done, slot=slot):
                    v.wait_ge(c.PE.h, mm_done)
                    pv3 = bank.rearrange("p (h k) -> p h k", k=64)
                    hh = 0 if g == 4 else 8
                    c.DVE.inc(v.tensor_copy(vr4[:, slot, hh:hh + 8, 0:64],
                                            pv3))
                c.ve(ev)
                evac_marks.append((c.DVE, c.DVE.n))
        qk_ready_act = evac_marks[3][1]

        qk4 = qk_t[:, 0:1024].rearrange("p (h z f) -> p h z f", z=2, f=32)
        kk4 = qk_t[:, 1024:2048].rearrange("p (h z f) -> p h z f", z=2, f=32)
        qr4 = qr_t[:].rearrange("p (h z f) -> p h z f", z=2, f=32)
        kr4 = kr_t[:].rearrange("p (h z f) -> p h z f", z=2, f=32)

        def rope_v(v, buf=buf, t=t, qka=qk_ready_act):
            v.wait_ge(c.ACT.h, qka)
            a = qk4[:, :, 0, :]
            bq = qk4[:, :, 1, :]
            oe = qr4[:, :, 0, :]
            oo = qr4[:, :, 1, :]
            cosv = cos_t[:, 512 * buf:512 * buf + 512].rearrange(
                "p (h f) -> p h f", f=32)
            sinv = sin_t[:, 512 * buf:512 * buf + 512].rearrange(
                "p (h f) -> p h f", f=32)
            s2a = sq2[:, 0:512].rearrange("p (h f) -> p h f", f=32)
            s2b = sq2[:, 512:1024].rearrange("p (h f) -> p h f", f=32)
            s2c = qbuf[:, 1024:1536].rearrange("p (h f) -> p h f", f=32)
            s2d = qbuf[:, 1536:2048].rearrange("p (h f) -> p h f", f=32)
            v.tensor_tensor(s2a, a, cosv, ALU.mult)
            v.tensor_tensor(s2b, bq, sinv, ALU.mult)
            v.tensor_tensor(s2c, bq, cosv, ALU.mult)
            c.DVE.inc(v.tensor_tensor(s2d, a, sinv, ALU.mult))
            v.drain()
            v.tensor_sub(oe, s2a, s2b)
            c.DVE.inc(v.tensor_add(oo, s2c, s2d))
        c.ve(rope_v)
        qrope_done = c.DVE.n

        def rope_p(g, buf=buf, qk_ready=qk_ready_act,
                   prev=(joins if t == 0 else None),
                   pnorm=(am[t - 1]["norms"] if t >= 1 else None)):
            wait_marks(c, g, prev)
            g.wait_ge(c.ACT.h, qk_ready)
            if pnorm is not None:
                g.wait_ge(c.DVE.h, pnorm)
            a = kk4[:, :, 0, :]
            bk = kk4[:, :, 1, :]
            oe = kr4[:, :, 0, :]
            oo = kr4[:, :, 1, :]
            cosv = cos_t[:, 512 * buf:512 * buf + 512].rearrange(
                "p (h f) -> p h f", f=32)
            sinv = sin_t[:, 512 * buf:512 * buf + 512].rearrange(
                "p (h f) -> p h f", f=32)
            qba = qbuf[:, 0:512].rearrange("p (h f) -> p h f", f=32)
            qbb = qbuf[:, 512:1024].rearrange("p (h f) -> p h f", f=32)
            sqa = sq[:, 0:512].rearrange("p (h f) -> p h f", f=32)
            sqb = sq[:, 512:1024].rearrange("p (h f) -> p h f", f=32)
            g.tensor_tensor(qba, a, cosv, ALU.mult)
            g.tensor_tensor(qbb, bk, sinv, ALU.mult)
            g.tensor_tensor(sqa, bk, cosv, ALU.mult)
            c.PL.inc(g.tensor_tensor(sqb, a, sinv, ALU.mult))
            g.drain()
            g.tensor_sub(oe, qba, qbb)
            c.PL.inc(g.tensor_add(oo, sqa, sqb))
        c.gp(rope_p)
        krope_done = c.PL.n

        def casts(s, qrope_done=qrope_done, krope_done=krope_done):
            s.wait_ge(c.DVE.h, qrope_done)
            c.ACT.inc(s.activation(qr_bf[:], qr_t[:], AF.Copy, scale=0.125))
            s.wait_ge(c.PL.h, krope_done)
            c.ACT.inc(s.activation(kr_bf[:], kr_t[:], AF.Copy))
        c.sc(casts)
        qsq_done = c.ACT.n - 1
        ksq_done = c.ACT.n

        def sqs(g, qrope_done=qrope_done):
            g.wait_ge(c.DVE.h, qrope_done)
            g.tensor_tensor(sq[:], qr_t[:], qr_t[:], ALU.mult)
            c.PL.inc(g.tensor_tensor(sq2[:], kr_t[:], kr_t[:], ALU.mult))
        c.gp(sqs)
        sq_pl = c.PL.n

        def norms(v, sq_pl=sq_pl, t=t):
            v.wait_ge(c.PL.h, sq_pl)
            s3 = sq[:].rearrange("p (h d) -> p h d", d=64)
            c.DVE.inc(v.tensor_reduce(qn2[:], s3, AX.X, ALU.add))
            s32 = sq2[:].rearrange("p (h d) -> p h d", d=64)
            c.DVE.inc(v.tensor_reduce(kn2s[:], s32, AX.X, ALU.add))
            v.drain()
            v.tensor_copy(qn2_bf[:], qn2[:])
            c.DVE.inc(v.tensor_copy(kn2_bf[:], kn2s[:]))
        c.ve(norms)
        norms_done = c.DVE.n

        def tn_qn(tn, norms_done=norms_done, xnfm_ready=xnfm_ready):
            tn.wait_ge(c.DVE.h, norms_done)
            tn.wait_ge(c.ACT.h, xnfm_ready)
            c.PE.inc(tn.transpose(pt_tr[0:16, 0:128], qn2_bf[:], ident[:]))
            c.PE.inc(tn.transpose(pt_tr[0:16, 128:256], kn2_bf[:], ident[:]))
        c.te(tn_qn)
        tqn_done = c.PE.n

        def redn(v, tqn_done=tqn_done, t=t):
            v.wait_ge(c.PE.h, tqn_done)
            c.DVE.inc(v.tensor_reduce(w16[:], pt_tr[0:16, 0:128], AX.X,
                                      ALU.max))
            c.DVE.inc(v.tensor_reduce(knhist[:, t:t + 1],
                                      pt_tr[0:16, 128:256], AX.X, ALU.max))
            v.drain()
            lo = max(0, t - 4)
            c.DVE.inc(v.tensor_reduce(w16b[:], knhist[:, lo:t + 1], AX.X,
                                      ALU.max))
            v.drain()
            c.DVE.inc(v.scalar_tensor_tensor(w16[:], w16[:], 1.0,
                                             w16b[:], ALU.mult, ALU.add))
        c.ve(redn)
        nbc_done = emit_negm_bcast(
            c, w16, w16c, nm_bf, nmrow, pt_tr,
            qv0[:], negm_bc[:, 16 * buf:16 * buf + 16], ident,
            ones_row, 1.0 / 16.0, c.DVE.n,
            bcm_waits=[(c.DVE, evac_marks[4][1])],
            af=AF.Copy)

        qfm_ready = emit_transpose8(
            c, qr_bf, pt_tr, ident,
            lambda buf=buf: (q_fm[:, 1024 * buf:1024 * buf + 1024], pt_tr[:]),
            pe_wait=[(c.ACT, qsq_done), (c.ACT, nbc_done)])
        kfm_ready = emit_transpose8(
            c, kr_bf, pt_tr, ident,
            lambda slot=slot: (krr[:, 1024 * slot:1024 * slot + 1024],
                               pt_tr[:]),
            pe_wait=[(c.ACT, qfm_ready)])

        xd_done = emit_quant_fp6(c, xn_bf, xd_bf, qb, consts, norms_done)
        xdfm_ready = emit_transpose8(
            c, xd_bf, pt_tr, ident,
            lambda x3=(t % 3): (xd_fm[:, 1024 * x3:1024 * x3 + 1024],
                                pt_tr[:]),
            pe_wait=[(c.DVE, xd_done), (c.ACT, kfm_ready)],
            evac_wait=([(c.PE, bm[t - 3]["fm1"])] if t >= 3 else None))

        am[t] = {"qadd": xd_done, "xn_tr": xn_tr_pe, "xdfm": xdfm_ready,
                 "kfm": kfm_ready, "qfm": qfm_ready, "nbc": nbc_done,
                 "em5": evac_marks[5], "qrope": qrope_done,
                 "krope": krope_done, "norms": norms_done}

    def emit_B1(t, next_nbc):
        buf3 = t % 3
        buf = t % 2
        a_t = am[t]
        pb = bm.get(t - 1)
        lo_tile = max(0, t - 4)
        nsl = t - lo_tile + 1
        qfm_s = q_fm[:, 1024 * buf:1024 * buf + 1024]
        nbc_s = negm_bc[:, 16 * buf:16 * buf + 16]
        exp_hist = []
        tail_hist = []
        pv_pe = []
        pend0 = None

        def mk_pvt(hp, nsl, lo_tile, exp_done, po, pr):
            def pv(tn, hp=hp, nsl=nsl, lo_tile=lo_tile, exp_done=exp_done,
                   po=po, pr=pr,
                   prev_tail=(tail_hist[hp - 1] if hp >= 1 else None),
                   xdp=(pb["tail7"] if (hp == 0 and pb) else None)):
                tn.wait_ge(c.ACT.h, exp_done)
                if prev_tail is not None:
                    tn.wait_ge(c.ACT.h, prev_tail)
                if xdp is not None:
                    tn.wait_ge(c.ACT.h, xdp)
                for j, (hd, pT) in enumerate(((2 * hp, pT_a),
                                              (2 * hp + 1, pT_b))):
                    for si in range(nsl):
                        sl = ((lo_tile + si) % 6)
                        v0 = HV * sl + VST * hd
                        i = tn.matmul(
                            ppv[0:65, pr + 128 * j:pr + 128 * j + 128],
                            vrr[:, v0:v0 + VST],
                            pT[:, po + 128 * si:po + 128 * si + 128],
                            start=(si == 0), stop=(si == nsl - 1))
                c.PE.inc(i)
            c.te(pv)
            pv_done = c.PE.n
            pv_pe.append(pv_done)

            def tailc(sx, hp=hp, pr=pr, pv_done=pv_done, buf=buf,
                      pw=((bm[t - 2]["wom2"] if (t >= 2 and hp == 0)
                           else None))):
                sx.wait_ge(c.PE.h, pv_done)
                if pw is not None:
                    sx.wait_ge(c.PE.h, pw)
                ao = 1024 * buf
                sx.activation(ao_fm[0:64, ao + 128 * hp:ao + 128 * hp + 128],
                              ppv[0:64, pr:pr + 128], AF.Copy)
                sx.activation(
                    ao_fm[64:128, ao + 128 * hp:ao + 128 * hp + 128],
                    ppv[0:64, pr + 128:pr + 256], AF.Copy)
                sx.activation(linv_all[0:1, 128 * hp:128 * hp + 128],
                              ppv[64:65, pr:pr + 128], AF.Copy)
                c.ACT.inc(sx.activation(
                    linv_all[64:65, 128 * hp:128 * hp + 128],
                    ppv[64:65, pr + 128:pr + 256], AF.Copy))
            c.sc(tailc)
            tail_hist.append(c.ACT.n)

        for hp in range(8):
            if hp >= 1:
                sc_prev = exp_hist[hp - 1]
            elif pb:
                sc_prev = pb["exps7"]
            else:
                sc_prev = None
            sc_dve = pb["ao_all"] if (hp == 0 and pb) else None

            def smm(tn, hp=hp, lo_tile=lo_tile, nsl=nsl, t=t,
                    kfm_ready=a_t["kfm"], prev=sc_prev, scd=sc_dve,
                    qfm_s=qfm_s,
                    mbw=(mb_ready if (t == 0 and hp == 0) else None)):
                tn.wait_ge(c.ACT.h, kfm_ready)
                if prev is not None:
                    tn.wait_ge(c.ACT.h, prev)
                if scd is not None:
                    tn.wait_ge(c.DVE.h, scd)
                if mbw is not None:
                    tn.wait_ge(c.DVE.h, mbw)
                for si in range(nsl):
                    sl = ((lo_tile + si) % 6)
                    k0 = 1024 * sl + 128 * hp
                    mb = (mbD if si == nsl - 1 else
                          (mbO if (si == 0 and t >= 4) else None))
                    i = tn.matmul(SC[:, 128 * si:128 * si + 128],
                                  krr[0:64, k0:k0 + 128],
                                  qfm_s[0:64, 128 * hp:128 * hp + 128],
                                  start=True, stop=(mb is None))
                    i = tn.matmul(SC[:, 768 + 128 * si:768 + 128 * si + 128],
                                  krr[64:128, k0:k0 + 128],
                                  qfm_s[64:128, 128 * hp:128 * hp + 128],
                                  start=True, stop=(mb is None),
                                  tile_position=(64, 0))
                    if mb is not None:
                        i = tn.matmul(SC[:, 128 * si:128 * si + 128],
                                      ident[:], mb[:],
                                      start=False, stop=True)
                        i = tn.matmul(
                            SC[:, 768 + 128 * si:768 + 128 * si + 128],
                            ident[:], mb[:], start=False, stop=True)
                c.PE.inc(i)
            c.te(smm)
            smm_done = c.PE.n

            po = 640 * (hp % 2)

            def ex(sx, hp=hp, nsl=nsl, smm_done=smm_done, po=po,
                   nbc_s=nbc_s,
                   pvprev=(pv_pe[hp - 2] if hp >= 2 else
                           (pb["pv6"] if (hp == 0 and pb) else
                            (pb["pv7"] if (hp == 1 and pb) else None)))):
                sx.wait_ge(c.PE.h, smm_done)
                if pvprev is not None:
                    sx.wait_ge(c.PE.h, pvprev)
                c.ACT.inc(sx.activation(pT_a[:, po:po + 128 * nsl],
                                        SC[:, 0:128 * nsl], AF.Exp,
                                        bias=nbc_s[:, 2 * hp:2 * hp + 1]))
                c.ACT.inc(sx.activation(
                    pT_b[:, po:po + 128 * nsl],
                    SC[:, 768:768 + 128 * nsl], AF.Exp,
                    bias=nbc_s[:, 2 * hp + 1:2 * hp + 2]))
            c.sc(ex)
            exp_hist.append(c.ACT.n)

            pr = 256 * (hp % 2)
            if pend0 is not None:
                mk_pvt(*pend0)
            pend0 = (hp, nsl, lo_tile, exp_hist[hp], po, pr)
        mk_pvt(*pend0)

        def bcast(tn, tail7=tail_hist[7], e7=exp_hist[7]):
            tn.wait_ge(c.ACT.h, tail7)
            tn.wait_ge(c.ACT.h, e7)
            for g2 in range(2):
                i = tn.matmul(SC[0:64, 512 * g2:512 * g2 + 512],
                              ones_all[0:1, 0:64],
                              linv_all[0:1, 512 * g2:512 * g2 + 512],
                              start=True, stop=True)
                i = tn.matmul(SC[64:128, 512 * g2:512 * g2 + 512],
                              ones_all[64:65, 0:64],
                              linv_all[64:65, 512 * g2:512 * g2 + 512],
                              start=True, stop=True, tile_position=(64, 64))
            c.PE.inc(i)
        c.te(bcast)
        bcast_done = c.PE.n

        def nrm(v, bcast_done=bcast_done, tail7=tail_hist[7], buf=buf):
            v.wait_ge(c.PE.h, bcast_done)
            v.wait_ge(c.ACT.h, tail7)
            c.DVE.inc(v.reciprocal(sc_bf[:], SC[:, 0:1024]))
            v.drain()
            ao = 1024 * buf
            c.DVE.inc(v.tensor_tensor(ao_fm[:, ao:ao + 1024],
                                      ao_fm[:, ao:ao + 1024], sc_bf[:],
                                      ALU.mult))
        c.ve(nrm)
        ao_all = c.DVE.n

        bm[t] = {"exps7": exp_hist[7], "pv6": pv_pe[6], "pv7": pv_pe[7],
                 "tail7": tail_hist[7], "ao_all": ao_all,
                 "smm7": smm_done}

    def emit_B2(t):
        buf3 = t % 3
        x3 = t % 3
        a_t = am[t]
        ao_all = bm[t]["ao_all"]

        x1ev_marks = []
        for g in range(2):
            def wff(tn, g=g, ao_all=ao_all, xdfm_ready=a_t["xdfm"],
                    x3=x3, buf3=buf3,
                    pfree=(bm[t - 1]["x1_done"] if t >= 1 else None)):
                if g == 0:
                    tn.wait_ge(c.DVE.h, ao_all)
                    tn.wait_ge(c.ACT.h, xdfm_ready)
                    if pfree is not None:
                        tn.wait_ge(c.ACT.h, pfree)
                else:
                    tn.wait_ge(c.ACT.h, x1ev_marks[0])
                tn.matmul(pq0[:], identF[:],
                          x_t[:, 1024 * buf3 + 512 * g:
                              1024 * buf3 + 512 * g + 512],
                          start=True, stop=False)
                ao = 1024 * (t % 2)
                for ch in range(8):
                    tn.matmul(
                        pq0[:], ao_fm[:, ao + 128 * ch:ao + 128 * ch + 128],
                        wo[:, 1024 * ch + 512 * g:1024 * ch + 512 * g + 512],
                        start=False, stop=False)
                for ch in range(8):
                    tn.matmul(
                        pq0[:],
                        xd_fm[:, 1024 * x3 + 128 * ch:
                              1024 * x3 + 128 * ch + 128],
                        wdq[:, 1024 * ch + 512 * g:1024 * ch + 512 * g + 512],
                        start=False, stop=False)
                i = tn.matmul(pq0[:], ones_row[0:1, 0:128],
                              fbias[0:1, 512 * g:512 * g + 512],
                              start=False, stop=True)
                c.PE.inc(i)
            c.te(wff)
            wff_done = c.PE.n
            if g == 1:
                wom_mark2 = wff_done

            def x1ev(sx, g=g, buf3=buf3, wff_done=wff_done):
                sx.wait_ge(c.PE.h, wff_done)
                c.ACT.inc(sx.activation(
                    x_t[:, 1024 * buf3 + 512 * g:
                        1024 * buf3 + 512 * g + 512],
                    pq0[:, 0:512], AF.Copy))
            c.sc(x1ev)
            x1ev_marks.append(c.ACT.n)
        x1_done = x1ev_marks[1]
        fm_done = wom_mark2

        def st(g, t=t, buf3=buf3, x1_done=x1_done, gsn=c.GDS.n):
            g.wait_ge(c.GDS.h, gsn)
            g.wait_ge(c.ACT.h, x1_done)
            c.GDS.inc(g.dma_start(
                out=x1[128 * t:128 * t + 128, :],
                in_=x_t[:, 1024 * buf3:1024 * buf3 + 1024]), 16)
        c.gp(st)
        st_counts[t] = c.GDS.n

        bm[t]["x1_done"] = x1_done
        bm[t]["fm1"] = fm_done
        bm[t]["wom2"] = wom_mark2
        if t + 1 < NT:
            emit_pre(t + 1)

    emit_LN(0)
    emit_A(0)
    emit_LN(1)
    emit_A(1)
    emit_pre(0)
    emit_B1(0, next_nbc=None)
    for r in range(2, NT):
        emit_LN(r)
        emit_B1(r - 1, next_nbc=None)
        emit_B2(r - 2)
        emit_A(r)
    emit_B1(NT - 1, next_nbc=None)
    emit_B2(NT - 2)
    emit_B2(NT - 1)
    st_counts = [st_counts[t] for t in range(NT)]

    return {"x1_store_gd": st_counts, "end": c.marks()}



def perm_qkv_cols(w):
    idx = np.arange(3 * D)
    for part in range(2):
        for h in range(H):
            base = 1024 * part + 64 * h
            idx[base:base + 32] = base + 2 * np.arange(32)
            idx[base + 32:base + 64] = base + 2 * np.arange(32) + 1
    return np.ascontiguousarray(w[:, idx])


def host_constants():
    ident = np.eye(128, dtype=np.float32).astype(ml_dtypes.bfloat16)
    pos = np.arange(S, dtype=np.float64)
    invf = (1.0 / (10000.0 ** (np.arange(32).astype(np.float32) / 32.0))
            ).astype(np.float32)
    ang = (pos[:, None] * invf[None, :].astype(np.float64))
    ang32 = (pos.astype(np.float32)[:, None] * invf[None, :]).astype(np.float64)
    cos = np.cos(ang32).astype(np.float32)
    sin = np.sin(ang32).astype(np.float32)
    cosr = np.tile(cos, (1, 16)).astype(np.float32)
    sinr = np.tile(sin, (1, 16)).astype(np.float32)
    r = np.arange(128)
    m_old = (r[:, None] >= r[None, :]).astype(np.float32)
    m_diag = (r[:, None] <= r[None, :]).astype(np.float32)
    m_old = m_old.astype(ml_dtypes.bfloat16)
    m_diag = m_diag.astype(ml_dtypes.bfloat16)
    ind16 = np.zeros((128, 128), np.float32)
    for h in range(16):
        c0 = h // 2
        rr0 = 64 * (h % 2)
        ind16[rr0:rr0 + 64, 16 * c0 + h] = 1.0
    ind16 = ind16.astype(ml_dtypes.bfloat16)
    return {"ident": ident, "cosr": cosr, "sinr": sinr, "m_old": m_old,
            "m_diag": m_diag, "ind16": ind16}


def build_l0_program(x1_kind="ExternalOutput"):
    nc = bass.Bass("TRN2", target_bir_lowering=False, debug=BASS_DEBUG)
    io = {}
    io["x"] = nc.dram_tensor("x", [S, D], F32, kind="ExternalInput").ap()
    io["w_qkv"] = nc.dram_tensor("w_qkv", [D, 3 * D], F32,
                                 kind="ExternalInput").ap()
    io["w_o"] = nc.dram_tensor("w_o", [D, D], F32, kind="ExternalInput").ap()
    io["fw"] = nc.dram_tensor("fw", [D, D], F32, kind="ExternalInput").ap()
    io["fb"] = nc.dram_tensor("fb", [1, D], F32, kind="ExternalInput").ap()
    io["cosr"] = nc.dram_tensor("cosr", [S, 512], F32,
                                kind="ExternalInput").ap()
    io["sinr"] = nc.dram_tensor("sinr", [S, 512], F32,
                                kind="ExternalInput").ap()
    io["ident"] = nc.dram_tensor("ident", [128, 128], BF16,
                                 kind="ExternalInput").ap()
    io["m_old"] = nc.dram_tensor("m_old", [128, 128], BF16,
                                 kind="ExternalInput").ap()
    io["m_diag"] = nc.dram_tensor("m_diag", [128, 128], BF16,
                                  kind="ExternalInput").ap()
    io["x1"] = nc.dram_tensor("x1", [S, D], F32, kind=x1_kind).ap()
    if DEBUG_TILE >= 0:
        io["dbg"] = {
            "d_xn": nc.dram_tensor("d_xn", [128, 1024], BF16, kind="ExternalOutput").ap(),
            "d_qk": nc.dram_tensor("d_qk", [128, 2048], F32, kind="ExternalOutput").ap(),
            "d_qr": nc.dram_tensor("d_qr", [128, 1024], F32, kind="ExternalOutput").ap(),
            "d_kr": nc.dram_tensor("d_kr", [128, 1024], F32, kind="ExternalOutput").ap(),
            "d_negm": nc.dram_tensor("d_negm", [128, 16], F32, kind="ExternalOutput").ap(),
            "d_pta": nc.dram_tensor("d_pta", [128, 640], BF16, kind="ExternalOutput").ap(),
            "d_ao": nc.dram_tensor("d_ao", [128, 1024], BF16, kind="ExternalOutput").ap(),
            "d_xd": nc.dram_tensor("d_xd", [128, 1024], BF16, kind="ExternalOutput").ap(),
            "d_qfm": nc.dram_tensor("d_qfm", [128, 1024], BF16, kind="ExternalOutput").ap(),
            "d_krr": nc.dram_tensor("d_krr", [128, 1024], BF16, kind="ExternalOutput").ap(),
            "d_vrr": nc.dram_tensor("d_vrr", [128, 1040], BF16, kind="ExternalOutput").ap(),
            "d_mean": nc.dram_tensor("d_mean", [128, 1], F32, kind="ExternalOutput").ap(),
            "d_var": nc.dram_tensor("d_var", [128, 1], F32, kind="ExternalOutput").ap(),
            "d_rstd": nc.dram_tensor("d_rstd", [128, 1], F32, kind="ExternalOutput").ap(),
            "d_msq": nc.dram_tensor("d_msq", [128, 1], F32, kind="ExternalOutput").ap(),
            "d_xt": nc.dram_tensor("d_xt", [128, 1024], F32, kind="ExternalOutput").ap(),
            "d_qn2": nc.dram_tensor("d_qn2", [128, 16], BF16, kind="ExternalOutput").ap(),
            "d_kn2": nc.dram_tensor("d_kn2", [128, 16], BF16, kind="ExternalOutput").ap(),
            "d_knh": nc.dram_tensor("d_knh", [16, 32], F32, kind="ExternalOutput").ap(),
            "d_w16": nc.dram_tensor("d_w16", [16, 1], F32, kind="ExternalOutput").ap(),
            "d_w16b": nc.dram_tensor("d_w16b", [16, 1], F32, kind="ExternalOutput").ap(),
            "d_ptqn": nc.dram_tensor("d_ptqn", [16, 256], F32, kind="ExternalOutput").ap(),
        }
    es = ExitStack()
    with es:
        with nc.Block() as block:
            c = Ctx(nc, block, es)
            marks = build_l0(nc, c, io, es)
            def fin(e, n=c.GDS.n, n2=c.GD.n):
                e.wait_ge(c.GDS.h, n)
                e.wait_ge(c.GD.h, n2)
            c.sy(fin)
            c.flush()
    return nc



def build_l1(nc, c, io, es, x1_gd=None, joins0=None):
    if DEBUG_L1:
        io["d3_qfm"] = nc.dram_tensor("d3_qfm", [128, 4096], BF16,
                                      kind="ExternalOutput").ap()
        io["d3_negm"] = nc.dram_tensor("d3_negm", [128, 16], F32,
                                       kind="ExternalOutput").ap()
        io["d3_acc"] = nc.dram_tensor("d3_acc", [128, 4096], BF16,
                                      kind="ExternalOutput").ap()
        io["d3_kmax"] = nc.dram_tensor("d3_kmax", [16, 1], F32,
                                       kind="ExternalOutput").ap()
        io["d3_kfm"] = nc.dram_tensor("d3_kfm", [128, 1024], BF16,
                                      kind="ExternalOutput").ap()
        io["d3_vt"] = nc.dram_tensor("d3_vt", [128, 1040], BF16,
                                     kind="ExternalOutput").ap()
    x1 = io["x1"]
    wqkv1 = io["wqkv1"]
    wg, wu, wdn = io["wg"], io["wu"], io["wdn"]
    identc, ind16c = io["ident"], io["ind16"]
    out = io["out"]
    xnfm_s = io["xnfm_s"]
    x2p = io["x2p"]

    sb = lambda n, sh, dt: es.enter_context(nc.sbuf_tensor(n, sh, dt))

    ident = sb("ident1", [128, 128], BF16)
    ind16 = sb("ind16s", [128, 128], BF16)
    ones_row = sb("ones1", [1, 128], BF16)

    def prep(e):
        if joins0 is not None:
            wait_marks(c, e, joins0)
        if x1_gd is not None:
            e.wait_ge(c.GDS.h, x1_gd[-1])
        c.LD.inc(e.dma_start(out=ident[:], in_=identc), 16)
        c.LD.inc(e.dma_start(out=ind16[:], in_=ind16c), 16)
    c.sy(prep)

    def prep2(g):
        if joins0 is not None:
            wait_marks(c, g, joins0)
        if x1_gd is not None:
            g.wait_ge(c.GDS.h, x1_gd[-1])
        c.PL.inc(g.memset(ones_row[:], 1.0))
    c.gp(prep2)
    prep_ld1 = c.LD.n
    joins = c.marks()

    es23 = ExitStack()
    sb23 = lambda n, sh, dt: es23.enter_context(nc.sbuf_tensor(n, sh, dt))
    kfm = sb23("kfm", [128, 8 * S], BF16)
    vtm = sb23("vtm", [128, NT * HV], BF16)
    v4 = vtm[:].rearrange("p (s h v) -> p s h v", s=NT, v=VST)
    kmax2 = sb23("kmax2", [16, 1], F32)

    def prep_ones(g):
        if joins0 is not None:
            wait_marks(c, g, joins0)
        c.PL.inc(g.memset(v4[:, :, :, 64:65], 1.0))
    c.gp(prep_ones)
    joins = c.marks()

    es2 = ExitStack()
    sb2 = lambda n, sh, dt: es2.enter_context(nc.sbuf_tensor(n, sh, dt))
    ps2 = lambda n, sh, dt: es2.enter_context(nc.psum_tensor(n, sh, dt))

    wkvT = sb2("wkvT", [128, 16 * 1024], BF16)
    wtmp = sb2("wtmp", [128, 1024], BF16)
    x1_t = sb2("x1_t", [128, 2048], F32)
    sqP = sb2("sqP", [128, 1024], F32)
    meanP = sb2("meanP", [128, 1], F32)
    varP = sb2("varP", [128, 1], F32)
    msqP = sb2("msqP", [128, 1], F32)
    rstdP = sb2("rstdP", [128, 1], F32)
    rstd2P = sb2("rstd2P", [128, 1], F32)
    xn_bfP = sb2("xn_bfP", [128, 1024], BF16)
    xnfm4 = sb2("xnfm4", [128, 4096], BF16)
    sqk = sb2("sqk", [128, 1024], BF16)
    kn16 = sb2("kn16", [16, 1], F32)

    pt_tr = ps2("pt_tr1", [128, 1024], BF16)
    pk0 = ps2("pk0", [128, 512], F32)
    pk1 = ps2("pk1", [128, 512], F32)
    pn = ps2("pn", [16, 512], F32)

    nmeanP = sb2("nmeanP", [128, 1], F32)
    lnbP = {"mean": meanP, "var": varP, "msq": msqP, "rstd": rstdP,
            "rstd2": rstd2P, "sq": sqP, "nmean": nmeanP}

    wkv3 = wkvT[:].rearrange("p (part cd k) -> p part cd k", part=2, k=1024)
    prev_tr_pe = c.PE.n
    prev_ev_act = c.ACT.n
    for m in range(16):
        def ld(g, m=m, ptp=prev_tr_pe, joins=joins):
            wait_marks(c, g, joins if m == 0 else None)
            if m >= 1:
                g.wait_ge(c.PE.h, ptp)
            c.GD.inc(g.dma_start(
                out=wtmp[:],
                in_=wqkv1[1024 + 128 * m:1024 + 128 * m + 128, :]), 16)
        c.gp(ld)
        ld_done = c.GD.n

        def tr(tn, ld_done=ld_done, prev_ev_act=prev_ev_act, joins=joins,
               m=m):
            wait_marks(c, tn, joins if m == 0 else None)
            if m == 0:
                tn.wait_ge(c.LD.h, prep_ld1)
            tn.wait_ge(c.GD.h, ld_done)
            if m >= 1:
                tn.wait_ge(c.ACT.h, prev_ev_act)
            for ch in range(8):
                i = tn.transpose(pt_tr[:, 128 * ch:128 * ch + 128],
                                 wtmp[:, 128 * ch:128 * ch + 128], ident[:])
            c.PE.inc(i)
        c.te(tr)
        prev_tr_pe = c.PE.n

        def ev(sx, m=m, tr_done=c.PE.n):
            sx.wait_ge(c.PE.h, tr_done)
            pv = pt_tr[:].rearrange("p (ch k) -> p ch k", k=128)
            part = 0 if m < 8 else 1
            c.ACT.inc(sx.activation(
                wkv3[:, part, :, 128 * (m % 8):128 * (m % 8) + 128], pv,
                AF.Copy))
        c.sc(ev)
        prev_ev_act = c.ACT.n
    wkv_ready = c.ACT.n

    vev_marks = {}
    kev_marks = {}
    knr_marks = {}
    p2_dve_ends = []
    p2_st = []
    joins2 = c.marks()
    ld_base = c.LD.n
    for t in range(NT):
        buf = t % 2
        tsub = t % 4

        def p2_load(e, t=t, buf=buf, joins2=joins2, ldn=c.LD.n):
            e.wait_ge(c.LD.h, ldn)
            if t >= 2:
                e.wait_ge(c.DVE.h, p2_dve_ends[t - 2])
                e.wait_ge(c.ACT.h, p2_act_ends_b[t - 2])
            c.LD.inc(e.dma_start(out=x1_t[:, 1024 * buf:1024 * buf + 1024],
                                 in_=x1[128 * t:128 * t + 128, :]), 16)
        c.sy(p2_load)
        ld_now = c.LD.n

        xn_ready = emit_layernorm(c, x1_t[:, 1024 * buf:1024 * buf + 1024],
                                  xn_bfP, lnbP, joins2, extra_ld=ld_now)

        pe_w = [(c.DVE, xn_ready)]
        if t == 0:
            pe_w.append((c.ACT, wkv_ready))
        if t >= 1:
            pe_w.append((c.GDS, p2_st[t - 1]))
        xv = xnfm4[:].rearrange("p (cd k) -> p cd k", k=512)

        def dstfn(tsub=tsub, xv=xv):
            pv = pt_tr[:].rearrange("p (ch k) -> p ch k", k=128)
            return (xv[:, :, 128 * tsub:128 * tsub + 128], pv)
        xnfm_done = emit_transpose8(c, xn_bfP, pt_tr, ident, dstfn,
                                    pe_wait=pe_w)

        def stx(g, t=t, tsub=tsub, xnfm_done=xnfm_done, xv=xv, gsn=c.GDS.n):
            g.wait_ge(c.GDS.h, gsn)
            g.wait_ge(c.ACT.h, xnfm_done)
            c.GDS.inc(g.dma_start(out=xnfm_s[128 * t:128 * t + 128, :],
                                  in_=xv[:, :, 128 * tsub:128 * tsub + 128]),
                      16)
        c.gp(stx)
        p2_st.append(c.GDS.n)

        for g in range(2):
            bank = pk0 if g == 0 else pk1

            def vm(tn, g=g, bank=bank, tsub=tsub, xnfm_done=xnfm_done, t=t):
                tn.wait_ge(c.ACT.h, xnfm_done)
                pm = vev_marks.get((t - 1, g))
                if pm is not None:
                    tn.wait_ge(c.DVE.h, pm)
                km_prev = kev_marks.get((t // 4 - 1, 6 + g))
                if km_prev is not None:
                    tn.wait_ge(c.ACT.h, km_prev)
                kn_prev = knr_marks.get(t // 4 - 1)
                if kn_prev is not None and g == 0:
                    tn.wait_ge(c.DVE.h, kn_prev)
                xv2 = xnfm4[:].rearrange("p (cd k) -> p cd k", k=512)
                for cd in range(8):
                    i = tn.matmul(
                        bank[:], xv2[:, cd, 128 * tsub:128 * tsub + 128],
                        wkvT[:, 8192 + 1024 * cd + 512 * g:
                             8192 + 1024 * cd + 512 * g + 512],
                        start=(cd == 0), stop=(cd == 7))
                c.PE.inc(i)
            c.te(vm)
            vm_done = c.PE.n

            def ev2(v, g=g, bank=bank, t=t, vm_done=vm_done):
                v.wait_ge(c.PE.h, vm_done)
                pv3 = bank[:].rearrange("p (h k) -> p h k", k=64)
                hh = 0 if g == 0 else 8
                c.DVE.inc(v.tensor_copy(v4[:, t, hh:hh + 8, 0:64], pv3))
            c.ve(ev2)
            vev_marks[(t, g)] = c.DVE.n

        if tsub == 3:
            g4 = t // 4
            for co in range(8):
                bank = pk0 if co % 2 == 0 else pk1

                def km(tn, co=co, bank=bank, g4=g4, t=t):
                    if co == 0:
                        tn.wait_ge(c.DVE.h, vev_marks[(t, 1)])
                    pm = kev_marks.get((g4, co - 2))
                    if pm is not None:
                        tn.wait_ge(c.ACT.h, pm)
                    for cd in range(8):
                        i = tn.matmul(
                            bank[:],
                            wkvT[:, 1024 * cd + 128 * co:
                                 1024 * cd + 128 * co + 128],
                            xnfm4[:, 512 * cd:512 * cd + 512],
                            start=(cd == 0), stop=(cd == 7))
                    c.PE.inc(i)
                c.te(km)
                km_done = c.PE.n

                def kev(sx, co=co, bank=bank, g4=g4, km_done=km_done):
                    sx.wait_ge(c.PE.h, km_done)
                    c.ACT.inc(sx.activation(
                        kfm[:, S * co + 512 * g4:S * co + 512 * g4 + 512],
                        bank[:], AF.Copy))
                c.sc(kev)
                kev_marks[(g4, co)] = c.ACT.n

            prev_ind = None
            for co in range(8):
                def ksq(sx, co=co, g4=g4, prev_ind=prev_ind):
                    sx.wait_ge(c.ACT.h, kev_marks[(g4, co)])
                    if prev_ind is not None:
                        sx.wait_ge(c.PE.h, prev_ind)
                    c.ACT.inc(sx.activation(
                        sqk[:, 512 * (co % 2):512 * (co % 2) + 512],
                        kfm[:, S * co + 512 * g4:S * co + 512 * g4 + 512],
                        AF.Square))
                c.sc(ksq)
                ksq_done = c.ACT.n

                def knm(tn, co=co, ksq_done=ksq_done, g4=g4):
                    tn.wait_ge(c.ACT.h, ksq_done)
                    if co == 0:
                        pk = knr_marks.get(g4 - 1)
                        if pk is not None:
                            tn.wait_ge(c.DVE.h, pk)
                    c.PE.inc(tn.matmul(
                        pn[:], ind16[:, 16 * co:16 * co + 16],
                        sqk[:, 512 * (co % 2):512 * (co % 2) + 512],
                        start=(co == 0), stop=(co == 7)))
                c.te(knm)
                prev_ind = c.PE.n
            kn_pe = c.PE.n

            def knr(v, g4=g4, kn_pe=kn_pe):
                v.wait_ge(c.PE.h, kn_pe)
                c.DVE.inc(v.tensor_reduce(kn16[:], pn[:], AX.X, ALU.max))
                v.drain()
                if g4 == 0:
                    c.DVE.inc(v.tensor_copy(kmax2[:], kn16[:]))
                else:
                    c.DVE.inc(v.tensor_tensor(kmax2[:], kmax2[:], kn16[:],
                                              ALU.max))
            c.ve(knr)
            knr_marks[g4] = c.DVE.n

        p2_dve_ends.append(c.DVE.n)
        try:
            p2_act_ends_b.append(c.ACT.n)
        except NameError:
            p2_act_ends_b = [c.ACT.n]
        joins2 = c.marks()
        if tsub == 3:
            joins2 = c.marks()

    p2_end = c.marks()
    p2_xn_st = list(p2_st)
    es2.close()

    es3 = ExitStack()
    sb3 = lambda n, sh, dt: es3.enter_context(nc.sbuf_tensor(n, sh, dt))
    ps3 = lambda n, sh, dt: es3.enter_context(nc.psum_tensor(n, sh, dt))

    wqT = sb3("wqT", [128, 8 * 1024], BF16)
    wtmp3 = sb3("wtmp3", [128, 1024], BF16)
    xnfmQ = sb3("xnfmQ", [128, 4096], BF16)
    qfm = sb3("qfm", [128, 4096], BF16)
    sqq = sb3("sqq", [128, 1024], BF16)
    qn16 = sb3("qn16", [16, 1], F32)
    m16o = sb3("m16o", [16, 1], F32)
    nmP = sb3("nmP", [16, 1], BF16)
    nmrowP = sb3("nmrowP", [1, 16], BF16)
    negm_bcP = sb3("negm_bcP", [128, 16], F32)
    pTA = sb3("pTA", [128, 3072], BF16)
    pTB = sb3("pTB", [128, 3072], BF16)
    pvs = sb3("pvs", [65, 512], BF16)
    attn_acc = sb3("attn_acc", [128, 4096], BF16)
    x12 = sb3("x12", [128, 3072], F32)
    linvP8 = sb3("linvP8", [128, 8], F32)

    psA = ps3("psA", [128, 1536], F32)
    psB = ps3("psB", [128, 1536], F32)
    pvA = ps3("pvA", [128, 512], F32)
    pvB = ps3("pvB", [128, 512], F32)
    pQ = psA[:, 0:512]
    pQn = psA[:, 512:1024]
    pQm = psA[:, 1024:1536]
    ptr3 = psB[:, 0:256].bitcast(BF16)
    ptrA = pvA[:, 0:256].bitcast(BF16)
    ptrB = pvB[:, 0:256].bitcast(BF16)

    wq3 = wqT[:].rearrange("p (cd k) -> p cd k", k=1024)
    prev_tr_pe = c.PE.n
    prev_ev_act = c.ACT.n
    joins3 = p2_end
    for m in range(8):
        def ld(g, m=m, ptp=prev_tr_pe, joins3=joins3):
            wait_marks(c, g, joins3 if m == 0 else None)
            if m >= 1:
                g.wait_ge(c.PE.h, ptp)
            c.GD.inc(g.dma_start(out=wtmp3[:],
                                 in_=wqkv1[128 * m:128 * m + 128, :]), 16)
        c.gp(ld)
        ld_done = c.GD.n

        for bb in range(2):
            def tr(tn, ld_done=ld_done, prev_ev_act=prev_ev_act, m=m,
                   bb=bb, joins3=joins3, prev_ev2=c.ACT.n):
                if m == 0 and bb == 0:
                    wait_marks(c, tn, joins3)
                tn.wait_ge(c.GD.h, ld_done)
                tn.wait_ge(c.ACT.h, prev_ev2)
                for ch in range(4):
                    i = tn.transpose(
                        ptr3[:, 128 * ch:128 * ch + 128],
                        wtmp3[:, 128 * (4 * bb + ch):
                              128 * (4 * bb + ch) + 128], ident[:])
                c.PE.inc(i)
            c.te(tr)

            def ev(sx, m=m, bb=bb, tr_done=c.PE.n):
                sx.wait_ge(c.PE.h, tr_done)
                pv = ptr3[:].rearrange("p (ch k) -> p ch k", k=128)
                c.ACT.inc(sx.activation(
                    wq3[:, 4 * bb:4 * bb + 4, 128 * m:128 * m + 128], pv,
                    AF.Copy))
            c.sc(ev)
        prev_tr_pe = c.PE.n
        prev_ev_act = c.ACT.n
    wq_ready = c.ACT.n

    joins3 = c.marks()
    x2p_st = []
    prev_qb_exp = None
    prev_qm7 = None
    ldq_marks = {}

    def emit_ldq(qb):
        def ldq(e, qb=qb, ldn=c.LD.n, pq7=prev_qm7):
            e.wait_ge(c.LD.h, ldn)
            if qb == 0:
                e.wait_ge(c.GDS.h, p2_xn_st[-1])
                wait_marks(c, e, p2_end)
            else:
                e.wait_ge(c.PE.h, pq7)
            xq = xnfmQ[:].rearrange("p (cd k) -> p cd k", k=512)
            for tsub in range(4):
                t0 = 4 * qb + tsub
                c.LD.inc(e.dma_start(
                    out=xq[:, :, 128 * tsub:128 * tsub + 128],
                    in_=xnfm_s[128 * t0:128 * t0 + 128, :]), 16)
        c.sy(ldq)
        ldq_marks[qb] = c.LD.n

    emit_ldq(0)
    for qb in range(NQB):
        ldq_done = ldq_marks[qb]

        qev_marks = []
        for co in range(8):
            qs = pQ if co % 2 == 0 else pQm

            def qm(tn, co=co, qs=qs, ldq_done=ldq_done, joins3=joins3,
                   qb=qb,
                   qprev=(qev_marks[co - 2] if co >= 2 else None),
                   pqe=prev_qb_exp):
                if co == 0:
                    wait_marks(c, tn, joins3)
                    tn.wait_ge(c.LD.h, ldq_done)
                    if pqe:
                        tn.wait_ge(c.ACT.h, pqe)
                    if qb == 0:
                        tn.wait_ge(c.ACT.h, wq_ready)
                if qprev is not None:
                    tn.wait_ge(c.ACT.h, qprev)
                for cd in range(8):
                    i = tn.matmul(
                        qs, wqT[:, 1024 * cd + 128 * co:
                                   1024 * cd + 128 * co + 128],
                        xnfmQ[:, 512 * cd:512 * cd + 512],
                        start=(cd == 0), stop=(cd == 7))
                c.PE.inc(i)
            c.te(qm)
            qm_done = c.PE.n

            def qev(sx, co=co, qs=qs, qm_done=qm_done):
                sx.wait_ge(c.PE.h, qm_done)
                c.ACT.inc(sx.activation(qfm[:, 512 * co:512 * co + 512],
                                        qs, AF.Copy, scale=0.125))
            c.sc(qev)
            qev_marks.append(c.ACT.n)
        prev_qm7 = qm_done

        qnm_marks = []
        pend_n = None

        def mk_qnm(co, qsq_done):
            def qnm(tn, co=co, qsq_done=qsq_done,
                    qv7=(qev_marks[7] if co == 0 else None)):
                tn.wait_ge(c.PL.h, qsq_done)
                if qv7 is not None:
                    tn.wait_ge(c.ACT.h, qv7)
                c.PE.inc(tn.matmul(
                    pQn[0:16, :], ind16[:, 16 * co:16 * co + 16],
                    sqq[:, 512 * (co % 2):512 * (co % 2) + 512],
                    start=(co == 0), stop=(co == 7)))
            c.te(qnm)
            qnm_marks.append(c.PE.n)

        for co in range(8):
            def qsq(g, co=co, qev=qev_marks[co],
                    pi=(qnm_marks[co - 2] if co >= 2 else None)):
                g.wait_ge(c.ACT.h, qev)
                if pi is not None:
                    g.wait_ge(c.PE.h, pi)
                c.PL.inc(g.tensor_tensor(
                    sqq[:, 512 * (co % 2):512 * (co % 2) + 512],
                    qfm[:, 512 * co:512 * co + 512],
                    qfm[:, 512 * co:512 * co + 512], ALU.mult))
            c.gp(qsq)
            if pend_n is not None:
                mk_qnm(*pend_n)
            pend_n = (co, c.PL.n)
        mk_qnm(*pend_n)
        qn_pe = c.PE.n

        def qnr(v, qn_pe=qn_pe):
            v.wait_ge(c.PE.h, qn_pe)
            c.DVE.inc(v.tensor_reduce(qn16[:], pQn[0:16, :], AX.X, ALU.max))
            v.drain()
            c.DVE.inc(v.scalar_tensor_tensor(qn16[:], qn16[:], 64.0,
                                             kmax2[:], ALU.mult, ALU.add))
        c.ve(qnr)
        nbc_done = emit_negm_bcast(c, qn16, m16o, nmP, nmrowP, ptr3, pQm,
                                   negm_bcP, ident, ones_row, 1.0 / 16.0,
                                   c.DVE.n,
                                   bcm_waits=([(c.ACT, prev_qb_exp)]
                                              if prev_qb_exp else ()),
                                   af=AF.Copy)

        prev_ts = joins3["DVE"]
        prev_hp_exp = None
        batches = [(3 * b, min(3 * b + 3, NT)) for b in range((NT + 2) // 3)]
        for hp in range(8):
            exp_marks = []
            pv_marks = []
            pend = None

            def mk_pvm(hp, b2, k0_, k1_, em, bi, prev_ts):
                def pvm_(tn, hp=hp, b2=b2, k0_=k0_, k1_=k1_, em=em,
                         prev_ts=prev_ts, bi=bi):
                    tn.wait_ge(c.ACT.h, em)
                    if bi == 0:
                        tn.wait_ge(c.DVE.h, prev_ts)
                    for kk in range(k0_, k1_):
                        off = 1536 * b2 + 512 * (kk - k0_)
                        v0 = HV * kk
                        i = tn.matmul(
                            pvA[0:65, :],
                            vtm[:, v0 + VST * 2 * hp:
                                v0 + VST * 2 * hp + VST],
                            pTA[:, off:off + 512],
                            start=(kk == 0), stop=(kk == NT - 1))
                        i = tn.matmul(
                            pvB[0:65, :],
                            vtm[:, v0 + VST * (2 * hp + 1):
                                v0 + VST * (2 * hp + 1) + VST],
                            pTB[:, off:off + 512],
                            start=(kk == 0), stop=(kk == NT - 1))
                    c.PE.inc(i)
                c.te(pvm_)
                pv_marks.append(c.PE.n)

            for bi, (k0_, k1_) in enumerate(batches):
                for kc in range(k0_, k1_):
                    sl = kc % 3

                    def smm(tn, hp=hp, kc=kc, sl=sl, nbc_done=nbc_done,
                            prev_exp=prev_hp_exp,
                            emv=(exp_marks[kc // 3 - 1] if kc >= 3 else None)):
                        if kc == 0 and hp == 0:
                            tn.wait_ge(c.ACT.h, nbc_done)
                        if kc < 3:
                            if prev_exp is not None:
                                tn.wait_ge(c.ACT.h, prev_exp)
                        if emv is not None:
                            tn.wait_ge(c.ACT.h, emv)
                        k0 = S * hp + 128 * kc
                        i = tn.matmul(psA[:, 512 * sl:512 * sl + 512],
                                      kfm[0:64, k0:k0 + 128],
                                      qfm[0:64, 512 * hp:512 * hp + 512],
                                      start=True, stop=True)
                        i = tn.matmul(psB[:, 512 * sl:512 * sl + 512],
                                      kfm[64:128, k0:k0 + 128],
                                      qfm[64:128, 512 * hp:512 * hp + 512],
                                      start=True, stop=True,
                                      tile_position=(64, 0))
                        c.PE.inc(i)
                    c.te(smm)
                    smm_done = c.PE.n

                b2 = bi % 2
                nb = k1_ - k0_

                def ex(sx, hp=hp, b2=b2, smm_done=smm_done, bi=bi, nb=nb,
                       pvm=(pv_marks[bi - 2] if bi >= 2 else None)):
                    sx.wait_ge(c.PE.h, smm_done)
                    if pvm is not None:
                        sx.wait_ge(c.PE.h, pvm)
                    c.ACT.inc(sx.activation(
                        pTA[:, 1536 * b2:1536 * b2 + 512 * nb],
                        psA[:, 0:512 * nb], AF.Exp,
                        bias=negm_bcP[:, 2 * hp:2 * hp + 1]))
                    c.ACT.inc(sx.activation(
                        pTB[:, 1536 * b2:1536 * b2 + 512 * nb],
                        psB[:, 0:512 * nb], AF.Exp,
                        bias=negm_bcP[:, 2 * hp + 1:2 * hp + 2]))
                c.sc(ex)
                exp_marks.append(c.ACT.n)

                if pend is not None:
                    mk_pvm(*pend)
                pend = (hp, b2, k0_, k1_, c.ACT.n, bi, prev_ts)
            mk_pvm(*pend)

            for j2, (pvx, ptrX, hd) in enumerate(
                    ((pvA, ptrA, 2 * hp), (pvB, ptrB, 2 * hp + 1))):
                def pev(sx, pvx=pvx, pvm=pv_marks[-1], j2=j2,
                        prev_dve=c.DVE.n):
                    sx.wait_ge(c.PE.h, pvm)
                    if j2 == 1:
                        sx.wait_ge(c.DVE.h, prev_dve)
                    c.ACT.inc(sx.activation(pvs[:], pvx[0:65, :], AF.Copy))
                c.sc(pev)
                pev_done = c.ACT.n

                def ptr(tn, ptrX=ptrX, pev_done=pev_done, prev_dve=c.DVE.n):
                    tn.wait_ge(c.ACT.h, pev_done)
                    tn.wait_ge(c.DVE.h, prev_dve)
                    for j in range(4):
                        i = tn.transpose(ptrX[:, 128 * j:128 * j + 65],
                                         pvs[0:65, 128 * j:128 * j + 128],
                                         ident[0:65, 0:65])
                    c.PE.inc(i)
                c.te(ptr)
                ptr_done = c.PE.n

                def nr(v, hd=hd, ptrX=ptrX, j2=j2, ptr_done=ptr_done):
                    v.wait_ge(c.PE.h, ptr_done)
                    o = 4 * j2
                    for j in range(4):
                        i = v.reciprocal(
                            linvP8[:, o + j:o + j + 1],
                            ptrX[:, 128 * j + 64:128 * j + 65])
                    c.DVE.inc(i)
                    v.drain()
                    for j in range(4):
                        i = v.tensor_scalar(
                            attn_acc[:, 1024 * j + 64 * hd:
                                     1024 * j + 64 * hd + 64],
                            ptrX[:, 128 * j:128 * j + 64],
                            linvP8[:, o + j:o + j + 1], None, ALU.mult)
                    c.DVE.inc(i)
                c.ve(nr)
            prev_ts = c.DVE.n
            prev_hp_exp = exp_marks[-1]
        attn_done = c.DVE.n
        prev_qb_exp = prev_hp_exp

        if qb + 1 < NQB:
            emit_ldq(qb + 1)

        for j in range(4):
            t0 = 4 * qb + j
            ji = 1024 * (j % 3)
            jo = 1024 * ((j + 2) % 3)

            def ldx(e, t0=t0, ji=ji, attn_done=attn_done, j=j,
                    st_prev=(x2p_st[-1] if len(x2p_st) >= 1 else None),
                    ldn=c.LD.n):
                e.wait_ge(c.LD.h, ldn)
                if j == 0:
                    e.wait_ge(c.DVE.h, attn_done)
                    if st_prev is not None:
                        e.wait_ge(c.GDS.h, st_prev)
                if j >= 2 and st_prev is not None:
                    e.wait_ge(c.GDS.h, st_prev)
                c.LD.inc(e.dma_start(
                    out=x12[:, ji:ji + 1024],
                    in_=x1[128 * t0:128 * t0 + 128, :]), 16)
            c.sy(ldx)
            ldx_done = c.LD.n

            def addx(v, j=j, ji=ji, jo=jo, ldx_done=ldx_done,
                     attn_done=attn_done):
                v.wait_ge(c.LD.h, ldx_done)
                v.wait_ge(c.DVE.h, attn_done)
                c.DVE.inc(v.tensor_add(
                    x12[:, jo:jo + 1024],
                    x12[:, ji:ji + 1024],
                    attn_acc[:, 1024 * j:1024 * j + 1024]))
            c.ve(addx)
            addx_done = c.DVE.n

            def stx2(g, t0=t0, jo=jo, addx_done=addx_done, gsn=c.GDS.n):
                g.wait_ge(c.GDS.h, gsn)
                g.wait_ge(c.DVE.h, addx_done)
                c.GDS.inc(g.dma_start(
                    out=x2p[128 * t0:128 * t0 + 128, :],
                    in_=x12[:, jo:jo + 1024]), 16)
            c.gp(stx2)
            x2p_st.append(c.GDS.n)
        if DEBUG_L1 and qb == 0:
            def dbg3(g, am=attn_done):
                g.wait_ge(c.DVE.h, am)
                c.GD.inc(g.dma_start(out=io["d3_qfm"], in_=qfm[:]), 16)
                c.GD.inc(g.dma_start(out=io["d3_negm"], in_=negm_bcP[:]), 16)
                c.GD.inc(g.dma_start(out=io["d3_acc"], in_=attn_acc[:]), 16)
                c.GD.inc(g.dma_start(out=io["d3_kmax"], in_=kmax2[:]), 16)
                c.GD.inc(g.dma_start(out=io["d3_kfm"], in_=kfm[:, 0:1024]), 16)
                c.GD.inc(g.dma_start(out=io["d3_vt"], in_=vtm[:, 0:1040]), 16)
            c.gp(dbg3)
        joins3 = c.marks()
        if DEBUG_L1 and qb == 0:
            joins3["GD"] = c.GD.n

    p3_end = c.marks()
    es3.close()
    es23.close()

    es4 = ExitStack()
    sb4 = lambda n, sh, dt: es4.enter_context(nc.sbuf_tensor(n, sh, dt))
    ps4 = lambda n, sh, dt: es4.enter_context(nc.psum_tensor(n, sh, dt))

    wgT = sb4("wgT", [128, 24576], BF16)
    wuT = sb4("wuT", [128, 24576], BF16)
    wdT = sb4("wdT", [128, 24576], BF16)
    hbuf = sb4("hbuf", [128, 3072], BF16)
    ubuf = sb4("ubuf", [128, 3072], BF16)
    hfm = sb4("hfm", [128, 3072], BF16)
    xnf4 = sb4("xnf4", [128, 2048], BF16)
    x2pt = sb4("x2pt", [128, 2048], F32)
    outt = sb4("outt", [128, 2048], F32)

    pg0 = ps4("pg0", [128, 512], F32)
    pg1 = ps4("pg1", [128, 512], F32)
    ptg = ps4("ptg", [128, 3072], BF16)
    pt_tr4 = ps4("pt_tr4", [128, 1024], BF16)

    es4w = ExitStack()
    wtmp_i = es4w.enter_context(nc.sbuf_tensor("wtmp_i", [128, 3072], I32))
    wtmp_b = es4w.enter_context(nc.sbuf_tensor("wtmp_b", [128, 3072], BF16))

    joins4 = p3_end
    prev_tr_pe = c.PE.n
    prev_cvt = c.DVE.n
    for wi, (wsrc, wdst, nchunk, blkw) in enumerate(
            ((wg, wgT, 24, 1024), (wu, wuT, 24, 1024), (wdn, wdT, 8, 3072))):
        wd3 = wdst[:].rearrange("p (cd k) -> p cd k", k=(3072 if blkw == 1024
                                                         else 1024))
        for m in range(nchunk):
            def ld(e, wsrc=wsrc, m=m, blkw=blkw, prev_cvt=prev_cvt,
                   joins4=joins4, first=(wi == 0 and m == 0),
                   stw=(x2p_st[-1] if x2p_st else 0)):
                if first:
                    wait_marks(c, e, joins4)
                    e.wait_ge(c.GDS.h, stw)
                e.wait_ge(c.DVE.h, prev_cvt)
                c.LDW.inc(e.dma_start(
                    out=wtmp_i[:, 0:blkw],
                    in_=wsrc[128 * m:128 * m + 128, :]), 16)
            c.sy(ld)
            ld_done = c.LDW.n

            def cvt(v, ld_done=ld_done, blkw=blkw, prev_tr=prev_tr_pe):
                v.wait_ge(c.LDW.h, ld_done)
                v.wait_ge(c.PE.h, prev_tr)
                c.DVE.inc(v.tensor_copy(wtmp_b[:, 0:blkw],
                                        wtmp_i[:, 0:blkw]))
            c.ve(cvt)
            prev_cvt = c.DVE.n

            nblk = blkw // 128
            for bb in range(0, nblk, 8):
                def tr(tn, bb=bb, nblk=nblk, cvt_done=prev_cvt,
                       prev_ev=c.ACT.n, first=(wi == 0 and m == 0 and bb == 0),
                       joins4=joins4):
                    if first:
                        wait_marks(c, tn, joins4)
                    tn.wait_ge(c.DVE.h, cvt_done)
                    tn.wait_ge(c.ACT.h, prev_ev)
                    for ch in range(bb, min(bb + 8, nblk)):
                        i = tn.transpose(
                            pt_tr4[:, 128 * (ch - bb):128 * (ch - bb) + 128],
                            wtmp_b[:, 128 * ch:128 * ch + 128], ident[:])
                    c.PE.inc(i)
                c.te(tr)
                tr_done = c.PE.n

                def ev(sx, m=m, bb=bb, nblk=nblk, wd3=wd3, tr_done=tr_done):
                    sx.wait_ge(c.PE.h, tr_done)
                    nn = min(8, nblk - bb)
                    pv = pt_tr4[:, 0:128 * nn].rearrange(
                        "p (ch k) -> p ch k", k=128)
                    c.ACT.inc(sx.activation(
                        wd3[:, bb:bb + nn, 128 * m:128 * m + 128], pv,
                        AF.Copy))
                c.sc(ev)
            prev_tr_pe = c.PE.n
    wff_ready = c.ACT.n
    es4w.close()

    joins4 = c.marks()
    out_st = []
    p4_dve_ends = []
    prev_hev = None
    for t in range(NT):
        buf = t % 2

        def ldt(e, t=t, buf=buf, joins4=joins4, ldn=c.LD.n):
            e.wait_ge(c.LD.h, ldn)
            if t == 0:
                wait_marks(c, e, joins4)
            if t >= 2:
                e.wait_ge(c.DVE.h, p4_dve_ends[t - 2])
                e.wait_ge(c.GDS.h, out_st[t - 1])
            c.LD.inc(e.dma_start(
                out=xnf4[:, 1024 * buf:1024 * buf + 1024],
                in_=xnfm_s[128 * t:128 * t + 128, :]), 16)
            c.LD.inc(e.dma_start(
                out=x2pt[:, 1024 * buf:1024 * buf + 1024],
                in_=x2p[128 * t:128 * t + 128, :]), 16)
        c.sy(ldt)
        ldt_done = c.LD.n

        gu_ev = []
        for half, (wmat, dst) in enumerate(((wgT, hbuf), (wuT, ubuf))):
            for g in range(6):
                bank = pg0 if (6 * half + g) % 2 == 0 else pg1

                def gm(tn, g=g, bank=bank, wmat=wmat, buf=buf, t=t,
                       ldt_done=ldt_done, half=half, joins4=joins4,
                       pdv=(p4_dve_ends[t - 1] if t >= 1 else None),
                       prev_ev=(gu_ev[6 * half + g - 2]
                                if 6 * half + g >= 2 else None)):
                    if half == 0 and g == 0:
                        tn.wait_ge(c.LD.h, ldt_done)
                        if t == 0:
                            wait_marks(c, tn, joins4)
                            tn.wait_ge(c.ACT.h, wff_ready)
                        else:
                            tn.wait_ge(c.DVE.h, pdv)
                    if prev_ev is not None:
                        tn.wait_ge(c.ACT.h, prev_ev)
                    for cd in range(8):
                        i = tn.matmul(
                            bank[:],
                            xnf4[:, 1024 * buf + 128 * cd:
                                 1024 * buf + 128 * cd + 128],
                            wmat[:, 3072 * cd + 512 * g:
                                 3072 * cd + 512 * g + 512],
                            start=(cd == 0), stop=(cd == 7))
                    c.PE.inc(i)
                c.te(gm)
                gm_done = c.PE.n

                def gev(sx, g=g, bank=bank, dst=dst, half=half,
                        gm_done=gm_done, prev_dve=c.DVE.n):
                    sx.wait_ge(c.PE.h, gm_done)
                    sx.wait_ge(c.DVE.h, prev_dve)
                    c.ACT.inc(sx.activation(
                        dst[:, 512 * g:512 * g + 512], bank[:],
                        AF.Silu if half == 0 else AF.Copy,
                        scale=(1.0 / 127.0 if half == 0 else 1.0)))
                c.sc(gev)
                gu_ev.append(c.ACT.n)
        gu_done = c.ACT.n

        hm3 = []
        htr3 = []
        hev3 = []
        for i3 in range(3):
            def hm(v, i3=i3, gw=gu_ev[7 + 2 * i3]):
                v.wait_ge(c.ACT.h, gw)
                c.DVE.inc(v.tensor_tensor(
                    hbuf[:, 1024 * i3:1024 * i3 + 1024],
                    hbuf[:, 1024 * i3:1024 * i3 + 1024],
                    ubuf[:, 1024 * i3:1024 * i3 + 1024], ALU.mult))
            c.ve(hm)
            hm3.append(c.DVE.n)

            def htr(tn, i3=i3, hmk=hm3[i3], prev_ev=prev_hev):
                tn.wait_ge(c.DVE.h, hmk)
                if i3 == 0 and prev_ev is not None:
                    tn.wait_ge(c.ACT.h, prev_ev)
                for ch in range(8 * i3, 8 * i3 + 8):
                    i = tn.transpose(ptg[:, 128 * ch:128 * ch + 128],
                                     hbuf[:, 128 * ch:128 * ch + 128],
                                     ident[:])
                c.PE.inc(i)
            c.te(htr)
            htr3.append(c.PE.n)

            def hev(sx, i3=i3, trk=htr3[i3]):
                sx.wait_ge(c.PE.h, trk)
                c.ACT.inc(sx.activation(
                    hfm[:, 1024 * i3:1024 * i3 + 1024],
                    ptg[:, 1024 * i3:1024 * i3 + 1024], AF.Copy))
            c.sc(hev)
            hev3.append(c.ACT.n)
        hm_done = hm3[-1]
        hev_done = hev3[-1]
        prev_hev = hev_done

        for g in range(2):
            bank = pg0 if g == 0 else pg1

            def dm(tn, g=g, bank=bank, hev3=tuple(hev3),
                   prev_dve=c.DVE.n):
                tn.wait_ge(c.DVE.h, prev_dve)
                for hc in range(24):
                    if g == 0 and hc % 8 == 0:
                        tn.wait_ge(c.ACT.h, hev3[hc // 8])
                    i = tn.matmul(
                        bank[:], hfm[:, 128 * hc:128 * hc + 128],
                        wdT[:, 1024 * hc + 512 * g:1024 * hc + 512 * g + 512],
                        start=(hc == 0), stop=(hc == 23))
                c.PE.inc(i)
            c.te(dm)
            dm_done = c.PE.n

            def dev(v, g=g, bank=bank, buf=buf, dm_done=dm_done,
                    ldt_done=ldt_done):
                v.wait_ge(c.PE.h, dm_done)
                v.wait_ge(c.LD.h, ldt_done)
                c.DVE.inc(v.scalar_tensor_tensor(
                    outt[:, 1024 * buf + 512 * g:1024 * buf + 512 * g + 512],
                    bank[:], 1.0 / (127.0 * 127.0),
                    x2pt[:, 1024 * buf + 512 * g:1024 * buf + 512 * g + 512],
                    ALU.mult, ALU.add))
            c.ve(dev)
        outt_done = c.DVE.n

        def sto(g, t=t, buf=buf, outt_done=outt_done, gsn=c.GDS.n):
            g.wait_ge(c.GDS.h, gsn)
            g.wait_ge(c.DVE.h, outt_done)
            c.GDS.inc(g.dma_start(
                out=out[128 * t:128 * t + 128, :],
                in_=outt[:, 1024 * buf:1024 * buf + 1024]), 16)
        c.gp(sto)
        out_st.append(c.GDS.n)
        p4_dve_ends.append(c.DVE.n)
    es4.close()
    return {"end": c.marks()}


def _add_l1_io(nc, io, x1_kind, inner_kinds):
    io["wqkv1"] = nc.dram_tensor("wqkv1", [3 * D, D], F32,
                                 kind="ExternalInput").ap()
    io["wg"] = nc.dram_tensor("wg", [HID, D], I32, kind="ExternalInput").ap()
    io["wu"] = nc.dram_tensor("wu", [HID, D], I32, kind="ExternalInput").ap()
    io["wdn"] = nc.dram_tensor("wdn", [D, HID], I32,
                               kind="ExternalInput").ap()
    io["ind16"] = nc.dram_tensor("ind16", [128, 128], BF16,
                                 kind="ExternalInput").ap()
    io["out"] = nc.dram_tensor("out", [S, D], F32,
                               kind="ExternalOutput").ap()
    kk = {"kind": "ExternalOutput"} if DEBUG_L1 else {}
    io["xnfm_s"] = nc.dram_tensor("xnfm_s", [S, D], BF16, **kk).ap()
    io["x2p"] = nc.dram_tensor("x2p", [S, D], F32, **kk).ap()


def build_l1_program():
    nc = bass.Bass("TRN2", target_bir_lowering=False)
    io = {}
    io["x1"] = nc.dram_tensor("x1", [S, D], F32, kind="ExternalInput").ap()
    io["ident"] = nc.dram_tensor("ident", [128, 128], BF16,
                                 kind="ExternalInput").ap()
    _add_l1_io(nc, io, "ExternalInput", "Internal")
    es = ExitStack()
    with es:
        with nc.Block() as block:
            c = Ctx(nc, block, es)
            build_l1(nc, c, io, es, x1_gd=None)
            def fin(e, n=c.GDS.n, n2=c.GD.n):
                e.wait_ge(c.GDS.h, n)
                e.wait_ge(c.GD.h, n2)
            c.sy(fin)
            c.flush()
    return nc


def build_full_program():
    nc = bass.Bass("TRN2", target_bir_lowering=False)
    io = {}
    io["x"] = nc.dram_tensor("x", [S, D], F32, kind="ExternalInput").ap()
    io["w_qkv"] = nc.dram_tensor("w_qkv", [D, 3 * D], F32,
                                 kind="ExternalInput").ap()
    io["w_o"] = nc.dram_tensor("w_o", [D, D], F32, kind="ExternalInput").ap()
    io["fw"] = nc.dram_tensor("fw", [D, D], F32, kind="ExternalInput").ap()
    io["fb"] = nc.dram_tensor("fb", [1, D], F32, kind="ExternalInput").ap()
    io["cosr"] = nc.dram_tensor("cosr", [S, 512], F32,
                                kind="ExternalInput").ap()
    io["sinr"] = nc.dram_tensor("sinr", [S, 512], F32,
                                kind="ExternalInput").ap()
    io["ident"] = nc.dram_tensor("ident", [128, 128], BF16,
                                 kind="ExternalInput").ap()
    io["m_old"] = nc.dram_tensor("m_old", [128, 128], BF16,
                                 kind="ExternalInput").ap()
    io["m_diag"] = nc.dram_tensor("m_diag", [128, 128], BF16,
                                  kind="ExternalInput").ap()
    io["x1"] = nc.dram_tensor("x1", [S, D], F32).ap()
    _add_l1_io(nc, io, None, None)
    es = ExitStack()
    with es:
        with nc.Block() as block:
            c = Ctx(nc, block, es)
            es0 = ExitStack()
            marks = build_l0(nc, c, io, es0)
            es0.close()
            build_l1(nc, c, io, es, x1_gd=marks["x1_store_gd"],
                     joins0=marks["end"])
            def fin(e, n=c.GDS.n, n2=c.GD.n):
                e.wait_ge(c.GDS.h, n)
                e.wait_ge(c.GD.h, n2)
            c.sy(fin)
            c.flush()
    return nc


_CONSTS = None


def kernel(L0_attn_x, L0_attn_w_qkv, L0_attn_w_o, L0_ffn_weight, L0_ffn_bias,
           L1_attn_W_qkv, L1_ffn_w_gate, L1_ffn_w_up, L1_ffn_w_down):
    global _CONSTS
    if _CONSTS is None:
        _CONSTS = host_constants()
    consts = _CONSTS
    x_full = np.ascontiguousarray(np.asarray(L0_attn_x, dtype=np.float32))
    shared = {
        "w_qkv": perm_qkv_cols(np.asarray(L0_attn_w_qkv, np.float32)),
        "w_o": np.ascontiguousarray(np.asarray(L0_attn_w_o, np.float32)),
        "fw": np.ascontiguousarray(np.asarray(L0_ffn_weight, np.float32)),
        "fb": np.ascontiguousarray(
            np.asarray(L0_ffn_bias, np.float32).reshape(1, D)),
        "wqkv1": np.ascontiguousarray(np.asarray(L1_attn_W_qkv, np.float32)),
        "wg": np.ascontiguousarray(np.asarray(L1_ffn_w_gate, np.int32)),
        "wu": np.ascontiguousarray(np.asarray(L1_ffn_w_up, np.int32)),
        "wdn": np.ascontiguousarray(np.asarray(L1_ffn_w_down, np.int32)),
        "cosr": consts["cosr"], "sinr": consts["sinr"],
        "ident": consts["ident"], "m_old": consts["m_old"],
        "m_diag": consts["m_diag"], "ind16": consts["ind16"],
    }
    if LAUNCH_PLAN == ("full",):
        nc = build_full_program()
        in_maps = [{**shared, "x": np.ascontiguousarray(x_full[b])}
                   for b in range(8)]
        res = run_bass_kernel_spmd(nc, in_maps, core_ids=list(range(8)))
        return np.stack([r["out"] for r in res.results], axis=0)
    else:
        nc0 = build_l0_program(x1_kind="ExternalOutput")
        l0_keys = ("x", "w_qkv", "w_o", "fw", "fb", "cosr", "sinr", "ident",
                   "m_old", "m_diag")
        in_maps = [{**{k: shared[k] for k in l0_keys if k in shared},
                    "x": np.ascontiguousarray(x_full[b])} for b in range(8)]
        res0 = run_bass_kernel_spmd(nc0, in_maps, core_ids=list(range(8)))
        x1s = [r["x1"] for r in res0.results]
        nc1 = build_l1_program()
        l1_keys = ("wqkv1", "wg", "wu", "wdn", "ident", "ind16")
        in_maps1 = [{**{k: shared[k] for k in l1_keys}, "x1": x1s[b]}
                    for b in range(8)]
        res1 = run_bass_kernel_spmd(nc1, in_maps1, core_ids=list(range(8)))
        return np.stack([r["out"] for r in res1.results], axis=0)


